# version 01 (final)
# baseline (speedup 1.0000x reference)
import math
import numpy as np
import ml_dtypes
from contextlib import ExitStack

import concourse.bass as bass
import concourse.mybir as mybir
from concourse.bass_utils import run_bass_kernel_spmd

F32 = mybir.dt.float32
BF16 = mybir.dt.bfloat16
FP16 = mybir.dt.float16
I32 = mybir.dt.int32
U16 = mybir.dt.uint16
AF = mybir.ActivationFunctionType
ALU = mybir.AluOpType
AX = mybir.AxisListType

NC = 8
SEQ = 8192
D = 2048
BLK = 512
NBLK = SEQ // BLK
TPC = 2 * BLK
NT = TPC // 128
ND = D // 128

LAMBDA_INIT = 0.8 - 0.6 * math.exp(-0.3)
L0_SHIFT = 32.0
LN_EPS = 1e-5
NPBF = np.dtype(ml_dtypes.bfloat16)


def blocks_of(c):
    return (c, NBLK - 1 - c)



class Sem:
    def __init__(self, ctx, nc, name):
        self.h = ctx.enter_context(nc.semaphore(name))
        self.n = 0


class Prog:
    ENGINES = ("sync", "vector", "scalar", "tensor", "gpsimd")

    def __init__(self, nc):
        self.nc = nc
        self.ctx = ExitStack()
        self.ctx.__enter__()
        self.items = {e: [] for e in self.ENGINES}
        self._consts = {}
        self._const_items = []
        self._const_sem = None

    def sb(self, name, shape, dt=F32):
        return self.ctx.enter_context(self.nc.sbuf_tensor(name, shape, dt))

    def ps(self, name, shape, dt=F32):
        return self.ctx.enter_context(self.nc.psum_tensor(name, shape, dt))

    def sem(self, name):
        return Sem(self.ctx, self.nc, name)

    def on(self, eng, builder, sem=None, k=1):
        if sem is None:
            self.items[eng].append(builder)
            return None

        def item(e, b=builder, s=sem, kk=k):
            b(e).then_inc(s.h, kk)

        self.items[eng].append(item)
        sem.n += k
        return sem.n

    def wait(self, eng, sem, val=None):
        v = sem.n if val is None else val
        if v > 0:
            self.items[eng].append(lambda e, s=sem, vv=v: e.wait_ge(s.h, vv))

    def constant(self, value, dt=F32):
        key = (dt, float(value))
        if key in self._consts:
            return
        t = self.sb(f"const{len(self._consts)}", [128, 1], dt)
        self._consts[key] = t
        self._const_items.append(lambda e, t=t, v=float(value): e.memset(t[:], v))
        self.nc.const_aps.register(dt, float(value), t[:])

    def emit_and_close(self):
        if self._const_items:
            cs = self.sem("const_rdy")
            pre = list(self._const_items)
            n = len(pre)

            def const_block(e, pre=pre, cs=cs):
                for f in pre:
                    f(e).then_inc(cs.h, 1)

            self.items["vector"] = [const_block] + self.items["vector"]
            for eng in ("scalar", "tensor", "gpsimd", "sync"):
                if self.items[eng]:
                    self.items[eng] = [lambda e, cs=cs, n=n: e.wait_ge(cs.h, n)] + self.items[eng]
        with self.nc.Block() as blk:
            for eng in self.ENGINES:
                lst = self.items[eng]
                if lst:
                    getattr(blk, eng)(lambda e, l=lst: [f(e) for f in l])
        self.ctx.__exit__(None, None, None)
        return self.nc


def new_prog():
    nc = bass.Bass("TRN2", target_bir_lowering=False)
    return Prog(nc)


def add_ident(pr):
    ones = pr.sb("id_ones", [128, 128], BF16)
    ident = pr.sb("ident", [128, 128], BF16)
    sv = pr.sem("sv_id")
    sg = pr.sem("sg_id")
    pr.on("vector", lambda e: e.memset(ones[:], 1.0), sv)
    pr.wait("gpsimd", sv)
    pr.on("gpsimd", lambda e: e.affine_select(
        ident[:], ones[:], [[-1, 128]], ALU.is_equal, 0.0,
        base=0, channel_multiplier=1), sg)
    pr.wait("tensor", sg)
    return ident



def build_l0_qkv(debug_xt=False):
    pr = new_prog()
    nc = pr.nc
    x_in = nc.dram_tensor("x", [TPC, D], F32, kind="ExternalInput").ap()
    w_in = nc.dram_tensor("wqkv", [D, 3 * D], F32, kind="ExternalInput").ap()
    qf_out = nc.dram_tensor("qf", [D, TPC], BF16, kind="ExternalOutput").ap()
    kf_out = nc.dram_tensor("kf", [D, TPC], BF16, kind="ExternalOutput").ap()
    v_out = nc.dram_tensor("v", [TPC, D], BF16, kind="ExternalOutput").ap()
    if debug_xt:
        xt_out = nc.dram_tensor("xt_dbg", [D, TPC], BF16, kind="ExternalOutput").ap()

    ident = add_ident(pr)
    pr.constant(LN_EPS)
    xt = [pr.sb(f"xt{a}", [128, TPC], BF16) for a in range(ND)]
    xbuf = [pr.sb(f"xbuf{i}", [128, D], F32) for i in range(2)]
    xh = [pr.sb(f"xh{i}", [128, D], BF16) for i in range(2)]
    sqd = pr.sb("sqd", [128, D], F32)
    wf = [pr.sb(f"wf{i}", [128, ND * BLK // 2], F32) for i in range(2)]
    wb = [pr.sb(f"wb{i}", [128, ND * BLK // 2], BF16) for i in range(2)]
    ostg = [pr.sb(f"ostg{i}", [128, TPC], BF16) for i in range(2)]
    vstg = [pr.sb(f"vstg{i}", [128, BLK], BF16) for i in range(2)]
    stat_sum = [pr.sb(f"stat_sum{i}", [128, 1]) for i in range(NT)]
    stat_sq = [pr.sb(f"stat_sq{i}", [128, 1]) for i in range(NT)]
    negmu = [pr.sb(f"negmu{i}", [128, 1]) for i in range(NT)]
    mu2 = [pr.sb(f"mu2{i}", [128, 1]) for i in range(NT)]
    var = [pr.sb(f"var{i}", [128, 1]) for i in range(NT)]
    rstd = [pr.sb(f"rstd{i}", [128, 1]) for i in range(NT)]

    tp_ps = [pr.ps(f"tp{i}", [128, 8 * 128], BF16) for i in range(2)]
    qk_ps = [pr.ps(f"qk{i}", [128, TPC]) for i in range(2)]
    v_ps = [pr.ps(f"vps{i}", [128, BLK]) for i in range(2)]

    ldx = [pr.sem("ldx0"), pr.sem("ldx1")]
    ldw = [pr.sem("ldw0"), pr.sem("ldw1")]
    sv = pr.sem("sv")
    ss = pr.sem("ss")
    st = pr.sem("st")
    soq = [pr.sem("soq0"), pr.sem("soq1")]
    sov = [pr.sem("sov0"), pr.sem("sov1")]
    sod = pr.sem("sod")

    NSLAB = 24
    SW = 256

    xh_free = [0, 0]
    x_free = [0, 0]
    tp_done = {}
    for i in range(NT):
        m = i % 2
        pr.wait("sync", sv, x_free[m])
        lx = pr.on("sync", lambda e, i=i, m=m: e.dma_start(
            out=xbuf[m][:], in_=x_in[128 * i:128 * (i + 1), :]), ldx[m], 16)
        pr.wait("vector", ldx[m], lx)
        nred = pr.on("vector", lambda e, m=m, i=i: e.reduce_sum(
            stat_sum[i][:], xbuf[m][:], axis=AX.X), sv)
        pr.wait("vector", sv, nred)
        pr.on("vector", lambda e, i=i: e.tensor_scalar(
            negmu[i][:], stat_sum[i][:], -1.0 / D, None, ALU.mult), sv)
        pr.wait("scalar", ldx[m], lx)
        pr.on("scalar", lambda e, m=m, i=i: e.activation(
            sqd[:], xbuf[m][:], AF.Square, scale=1.0 / math.sqrt(D), accum_out=stat_sq[i][:]))
        pr.wait("scalar", sv)
        pr.on("scalar", lambda e, i=i: e.activation(mu2[i][:], negmu[i][:], AF.Square), ss)
        pr.wait("vector", ss)
        nvar = pr.on("vector", lambda e, i=i: e.tensor_tensor(
            var[i][:], stat_sq[i][:], mu2[i][:], ALU.subtract), sv)
        pr.wait("scalar", sv)
        nln = pr.on("scalar", lambda e, i=i: e.activation(
            rstd[i][:], var[i][:], AF.Ln, bias=LN_EPS), ss)
        pr.wait("scalar", ss, nln)
        pr.on("scalar", lambda e, i=i: e.activation(rstd[i][:], rstd[i][:], AF.Exp, scale=-0.5), ss)
        pr.wait("vector", ss)
        pr.wait("vector", st, xh_free[m])
        xhn = pr.on("vector", lambda e, m=m, i=i: e.tensor_scalar(
            xh[m][:], xbuf[m][:], negmu[i][:], rstd[i][:], ALU.add, ALU.mult), sv)
        x_free[m] = xhn
        pr.wait("tensor", sv, xhn)
        for grp in range(2):
            pr.wait("tensor", sv, tp_done.get((i, grp - 1) if grp else (i - 1, 1), 0))
            for a8 in range(8):
                a = grp * 8 + a8
                last = a8 == 7
                pr.on("tensor", lambda e, m=m, a=a, a8=a8, grp=grp: e.transpose(
                    tp_ps[grp][:, 128 * a8:128 * (a8 + 1)],
                    xh[m][:, 128 * a:128 * (a + 1)], ident[:]),
                    st if last else None)
            tg = st.n
            pr.wait("vector", st, tg)
            for a8 in range(8):
                a = grp * 8 + a8
                pr.on("vector", lambda e, i=i, a=a, a8=a8, grp=grp: e.tensor_copy(
                    xt[a][:, 128 * i:128 * (i + 1)],
                    tp_ps[grp][:, 128 * a8:128 * (a8 + 1)]),
                    sv if a8 == 7 else None)
            tp_done[(i, grp)] = sv.n
        xh_free[m] = st.n
    xt_done = sv.n

    if debug_xt:
        pr.wait("gpsimd", sv, xt_done)
        for a in range(ND):
            pr.on("gpsimd", lambda e, a=a: e.dma_start(
                out=xt_out[128 * a:128 * (a + 1), :], in_=xt[a][:]), sod, 16)

    ostg_free = [0, 0]
    vstg_free = [0, 0]
    w_free = [0, 0]
    wf_free = [0, 0]
    qkev = {}
    vev = {}
    pr.wait("tensor", sv, xt_done)
    for s in range(NSLAB):
        m = s % 2
        pr.wait("sync", ss, wf_free[m])
        lw = pr.on("sync", lambda e, s=s, m=m: e.dma_start(
            out=wf[m][:].rearrange("p (a o) -> p a o", a=ND),
            in_=w_in[:, SW * s:SW * (s + 1)].rearrange("(a p) o -> p a o", p=128)), ldw[m], 16)
        pr.wait("scalar", ldw[m], lw)
        pr.wait("scalar", st, w_free[m])
        cw = pr.on("scalar", lambda e, m=m: e.copy(wb[m][:], wf[m][:]), ss)
        wf_free[m] = cw
        pr.wait("tensor", ss, cw)
        wbv = wb[m][:].rearrange("p (a o) -> p a o", a=ND)
        if s < 16:
            for oc in range(2):
                j = 2 * s + oc
                jm = j % 2
                pr.wait("tensor", sv, qkev.get(j - 2, 0))
                for a in range(ND):
                    for h in range(2):
                        pr.on("tensor", lambda e, a=a, oc=oc, h=h, jm=jm, wbv=wbv: e.matmul(
                            qk_ps[jm][:, BLK * h:BLK * (h + 1)],
                            wbv[:, a, 128 * oc:128 * (oc + 1)],
                            xt[a][:, BLK * h:BLK * (h + 1)],
                            start=(a == 0), stop=(a == ND - 1)),
                            st if (a == ND - 1 and h == 1) else None)
                tq = st.n
                pr.wait("vector", st, tq)
                pr.wait("vector", soq[jm], ostg_free[jm])
                ev = pr.on("vector", lambda e, jm=jm: e.tensor_copy(ostg[jm][:], qk_ps[jm][:]), sv)
                qkev[j] = ev
                col = 128 * j
                if col < 1024:
                    dst = qf_out[col:col + 128, :]
                elif col < 2048:
                    dst = kf_out[col - 1024:col - 1024 + 128, :]
                elif col < 3072:
                    dst = qf_out[col - 1024:col - 1024 + 128, :]
                else:
                    dst = kf_out[col - 2048:col - 2048 + 128, :]
                pr.wait("gpsimd", sv, ev)
                ostg_free[jm] = pr.on("gpsimd", lambda e, dst=dst, jm=jm: e.dma_start(out=dst, in_=ostg[jm][:]), soq[jm], 16)
        else:
            vs = s - 16
            for t in range(NT):
                jm = (vs * NT + t) % 2
                pr.wait("tensor", sv, vev.get((vs * NT + t) - 2, 0))
                for a in range(ND):
                    pr.on("tensor", lambda e, a=a, t=t, jm=jm, wbv=wbv: e.matmul(
                        v_ps[jm][:, 0:SW], xt[a][:, 128 * t:128 * (t + 1)],
                        wbv[:, a, :],
                        start=(a == 0), stop=(a == ND - 1)),
                        st if a == ND - 1 else None)
                tv = st.n
                pr.wait("vector", st, tv)
                pr.wait("vector", sov[jm], vstg_free[jm])
                ev = pr.on("vector", lambda e, jm=jm: e.tensor_copy(vstg[jm][:, 0:SW], v_ps[jm][:, 0:SW]), sv)
                vev[vs * NT + t] = ev
                pr.wait("gpsimd", sv, ev)
                vstg_free[jm] = pr.on("gpsimd", lambda e, t=t, vs=vs, jm=jm: e.dma_start(
                    out=v_out[128 * t:128 * (t + 1), SW * vs:SW * (vs + 1)],
                    in_=vstg[jm][:, 0:SW]), sov[jm], 16)
        w_free[m] = st.n

    for s_ in (soq[0], soq[1], sov[0], sov[1], sod):
        pr.wait("gpsimd", s_)
    return pr.emit_and_close()



def build_l0_attn(core, debug=False, debug_qb=0, qb_only=None):
    pr = new_prog()
    nc = pr.nc
    if debug:
        dbg_o2 = nc.dram_tensor("dbg_o2", [16 * 128, BLK], F32, kind="ExternalOutput").ap()
        dbg_l = nc.dram_tensor("dbg_l", [2 * 16, BLK], F32, kind="ExternalOutput").ap()
        dbg_dff = nc.dram_tensor("dbg_dff", [16 * 128, BLK], BF16, kind="ExternalOutput").ap()
    qf_in = nc.dram_tensor("qf", [D, TPC], BF16, kind="ExternalInput").ap()
    kf_in = nc.dram_tensor("kf", [D, SEQ], BF16, kind="ExternalInput").ap()
    v_in = nc.dram_tensor("v", [SEQ, D], BF16, kind="ExternalInput").ap()
    lam_in = {n: nc.dram_tensor(n, [1, 64], F32, kind="ExternalInput").ap()
              for n in ("lq1", "lk1", "lq2", "lk2")}
    wp_in = nc.dram_tensor("wproj", [D, D], F32, kind="ExternalInput").ap()
    x_in = nc.dram_tensor("x", [TPC, D], F32, kind="ExternalInput").ap()
    x1_out = nc.dram_tensor("x1", [TPC, D], F32, kind="ExternalOutput").ap()
    o1_scr = nc.dram_tensor("o1_scr", [16 * 128, BLK], F32, kind="Internal").ap()

    pr.constant(-L0_SHIFT)
    pr.constant(LN_EPS)
    pr.constant(float(math.log(1.0 - LAMBDA_INIT)))

    SEGK = 5
    NH = 16
    PSL = 256
    qsb = [pr.sb(f"q{a}", [128, BLK], BF16) for a in range(8)]
    kseg = [pr.sb(f"kseg{i}", [128, 8 * SEGK * 128], BF16) for i in range(2)]
    vseg = [pr.sb(f"vseg{i}", [128, SEGK * D], BF16) for i in range(2)]
    pt = [pr.sb(f"pt{i}", [128, 1024], BF16) for i in range(2)]
    osb = [pr.sb(f"osb{h}", [128, BLK], F32) for h in range(NH)]
    lsb = [pr.sb(f"lsb{m}", [16, BLK], F32) for m in range(2)]
    lstg = [pr.sb(f"lstg{i}", [1, BLK], F32) for i in range(2)]
    msc = pr.sb("msc", [16, BLK], F32)
    msstg = pr.sb("msstg", [1, BLK], F32)
    rstdc = pr.sb("rstdc", [16, BLK], F32)
    lamv = {n: pr.sb(f"lv_{n}", [1, 64], F32) for n in lam_in}
    lam_t = pr.sb("lam_t", [1, 4], F32)
    lam16 = pr.sb("lam16", [16, 1], F32)
    ones_col_b = pr.sb("ones_col_b", [128, 1], BF16)
    ones_col_f = pr.sb("ones_col_f", [128, 1], F32)
    ones_row1 = pr.sb("ones_row1", [1, 128], F32)
    eoncs = pr.sb("eoncs", [16, 16 * 128], F32)
    o1h = [pr.sb(f"o1h{i}", [128, BLK], F32) for i in range(2)]
    bc = [pr.sb(f"bc{i}", [128, BLK], F32) for i in range(2)]
    dfn = pr.sb("dfn", [128, BLK], F32)
    dfn2 = pr.sb("dfn2", [128, BLK], F32)
    dff = [pr.sb(f"dff{h}", [128, BLK], BF16) for h in range(NH)]
    wpf = pr.sb("wpf", [128, ND * PSL], F32)
    wpb = pr.sb("wpb", [128, ND * PSL], BF16)
    xsl = [pr.sb(f"xsl{i}", [128, PSL], F32) for i in range(2)]
    x1st = [pr.sb(f"x1st{i}", [128, PSL], F32) for i in range(2)]

    A = pr.ps("arenaA", [128, 2048])
    B = pr.ps("arenaB", [128, 2048])
    s_ps = [A[:, 0:1024], A[:, 1024:2048]]
    o_ps = [B[:, 0:BLK], B[:, BLK:1024]]
    l_ps = [B[:, 1024:1536], B[:, 1536:2048]]
    bc_ps = [A[:, 0:BLK], A[:, BLK:1024]]
    ms_ps = A[:, 1024:1536]
    lam_ps = A[:, 1536:2048]
    pj_ps = [B[:, 0:BLK], B[:, BLK:1024]]

    ld_lam = pr.sem("ld_lam")
    ld_q = pr.sem("ld_q")
    ld_k = [pr.sem("ld_k0"), pr.sem("ld_k1")]
    ld_v = [pr.sem("ld_v0"), pr.sem("ld_v1")]
    ld_o1 = [pr.sem("ld_o10"), pr.sem("ld_o11")]
    ld_xs = [pr.sem("ld_xs0"), pr.sem("ld_xs1")]
    ld_wp = pr.sem("ld_wp")
    sv = pr.sem("sv")
    ss = pr.sem("ss")
    st = pr.sem("st")
    sp = pr.sem("sp")
    so_osb = pr.sem("so_osb")
    so_lacc = pr.sem("so_lacc")
    so_ms = pr.sem("so_ms")
    so_x1 = [pr.sem("so_x10"), pr.sem("so_x11")]
    so_dbg = pr.sem("so_dbg")

    for n, ap in lam_in.items():
        pr.on("sync", lambda e, n=n, ap=ap: e.dma_start(out=lamv[n][:], in_=ap), ld_lam, 16)
    ldl = ld_lam.n
    pr.on("vector", lambda e: e.memset(ones_col_b[:], 1.0))
    pr.on("vector", lambda e: e.memset(ones_col_f[:], 1.0))
    pr.on("vector", lambda e: e.memset(ones_row1[:], 1.0))
    pr.on("vector", lambda e: e.memset(eoncs[:], 1.0), sv)
    ones_rdy = sv.n
    pr.wait("gpsimd", sv, ones_rdy)
    egn = pr.on("gpsimd", lambda e: e.affine_select(
        eoncs[:].rearrange("p (h j) -> p h j", h=16), eoncs[:].rearrange("p (h j) -> p h j", h=16),
        [[-1, 16], [0, 128]], ALU.is_equal, 0.0, base=0, channel_multiplier=1), sp)
    pr.wait("vector", ld_lam, ldl)
    pr.on("vector", lambda e: e.tensor_tensor(lamv["lq1"][:], lamv["lq1"][:], lamv["lk1"][:], ALU.mult), sv)
    pr.on("vector", lambda e: e.tensor_tensor(lamv["lq2"][:], lamv["lq2"][:], lamv["lk2"][:], ALU.mult), sv)
    pr.wait("vector", sv)
    pr.on("vector", lambda e: e.reduce_sum(lam_t[:, 0:1], lamv["lq1"][:], axis=AX.X), sv)
    pr.on("vector", lambda e: e.reduce_sum(lam_t[:, 1:2], lamv["lq2"][:], axis=AX.X), sv)
    pr.wait("scalar", sv)
    pr.on("scalar", lambda e: e.activation(lam_t[:, 0:1], lam_t[:, 0:1], AF.Exp), ss)
    pr.on("scalar", lambda e: e.activation(lam_t[:, 1:2], lam_t[:, 1:2], AF.Exp), ss)
    pr.wait("vector", ss)
    pr.on("vector", lambda e: e.tensor_tensor(lam_t[:, 2:3], lam_t[:, 0:1], lam_t[:, 1:2], ALU.subtract), sv)
    pr.wait("vector", sv)
    pr.on("vector", lambda e: e.tensor_scalar(lam_t[:, 2:3], lam_t[:, 2:3], LAMBDA_INIT, None, ALU.add), sv)
    lam_rdy = sv.n
    pr.wait("tensor", sv, lam_rdy)
    pr.on("tensor", lambda e: e.matmul(lam_ps[:, 0:1], ones_row1[:], lam_t[:, 2:3],
                                       start=True, stop=True), st)
    pr.wait("vector", st)
    pr.on("vector", lambda e: e.tensor_copy(lam16[:], lam_ps[0:16, 0:1]), sv)
    lam16_rdy = sv.n
    pr.wait("tensor", sv, lam16_rdy)

    s_free = [0, 0]
    pt_free = [0, 0]
    kv_free = [0, 0]
    o_free = [0, 0]
    l_free = [0, 0]
    lstg_free = [0, 0]
    q_free = [0]
    pt_par = [0]
    seg_par = [0]

    blocks = blocks_of(core)
    for qbi, b in enumerate(blocks):
        if qb_only is not None and qbi != qb_only:
            continue
        g0 = BLK * b
        nkt = 4 * (b + 1)
        osb_rdy = {}
        osb_stored = {}
        osb_stored_all = 0
        pr.wait("vector", so_lacc)
        z0 = pr.on("vector", lambda e: e.memset(lsb[0][:], 0.0), sv)
        z1 = pr.on("vector", lambda e: e.memset(lsb[1][:], 0.0), sv)
        lsb_zeroed = sv.n
        for m in range(2):
            pr.wait("sync", st, q_free[0])
            lq = 0
            for a in range(8):
                lq = pr.on("sync", lambda e, a=a, m=m, qbi=qbi: e.dma_start(
                    out=qsb[a][:], in_=qf_in[1024 * m + 128 * a:1024 * m + 128 * (a + 1),
                                            BLK * qbi:BLK * (qbi + 1)]), ld_q, 16)
            segs = [(s0, min(s0 + SEGK, nkt)) for s0 in range(0, nkt, SEGK)]
            for si, (kt0, kt1) in enumerate(segs):
                nk = kt1 - kt0
                w = 128 * nk
                sgp = seg_par[0] % 2
                seg_par[0] += 1
                pr.wait("sync", st, kv_free[sgp])
                lk = pr.on("sync", lambda e, kt0=kt0, w=w, m=m, sgp=sgp: e.dma_start(
                    out=kseg[sgp][:].rearrange("p (a w) -> p a w", a=8)[:, :, 0:w],
                    in_=kf_in[1024 * m:1024 * (m + 1), 128 * kt0:128 * kt0 + w]
                    .rearrange("(a p) w -> p a w", p=128)), ld_k[sgp], 16)
                lv = pr.on("sync", lambda e, kt0=kt0, nk=nk, sgp=sgp: e.dma_start(
                    out=vseg[sgp][:].rearrange("p (kv d) -> p kv d", d=D)[:, 0:nk, :],
                    in_=v_in[128 * kt0:128 * (kt0 + nk), :].rearrange("(kv p) d -> p kv d", p=128)), ld_v[sgp], 16)
                for h in range(NH):
                    hp = h % 2
                    base = 64 * (h % 2)
                    atile = h // 2
                    kts = list(range(kt0, kt1))
                    groups = [kts[i:i + 2] for i in range(0, len(kts), 2)]
                    for grp in groups:
                        jj = pt_par[0] % 2
                        pt_par[0] += 1
                        pr.wait("tensor", ld_q, lq)
                        pr.wait("tensor", ld_k[sgp], lk)
                        pr.wait("tensor", ld_v[sgp], lv)
                        pr.wait("tensor", ss, s_free[jj])
                        for gi, kt in enumerate(grp):
                            pr.on("tensor", lambda e, gi=gi, kt=kt, kt0=kt0, jj=jj, sgp=sgp,
                                  base=base, atile=atile: e.matmul(
                                      s_ps[jj][:, 512 * gi:512 * (gi + 1)],
                                      kseg[sgp][:].rearrange("p (a w) -> p a w", a=8)
                                      [base:base + 64, atile, 128 * (kt - kt0):128 * (kt - kt0) + 128],
                                      qsb[atile][base:base + 64, :],
                                      start=True, stop=True),
                                  st if gi == len(grp) - 1 else None)
                        smm = st.n
                        pr.wait("scalar", st, max(smm, pt_free[jj]))
                        ex = pr.on("scalar", lambda e, jj=jj, n=len(grp): e.activation(
                            pt[jj][:, 0:512 * n], s_ps[jj][:, 0:512 * n], AF.Exp,
                            scale=0.125, bias=-L0_SHIFT), ss)
                        s_free[jj] = ex
                        mask_mark = 0
                        for gi, kt in enumerate(grp):
                            if kt >= 4 * b:
                                pr.wait("gpsimd", ss, ex)
                                mask_mark = pr.on("gpsimd", lambda e, jj=jj, gi=gi, kt=kt, g0=g0: e.affine_select(
                                    pt[jj][:, 512 * gi:512 * (gi + 1)],
                                    pt[jj][:, 512 * gi:512 * (gi + 1)],
                                    [[1, BLK]], ALU.is_ge, 0.0,
                                    base=g0 - 128 * kt, channel_multiplier=-1), sp)
                        pr.wait("tensor", ss, ex)
                        if mask_mark:
                            pr.wait("tensor", sp, mask_mark)
                        if grp is groups[0]:
                            pr.wait("tensor", sv, max(o_free[hp], l_free[hp]))
                        for gi, kt in enumerate(grp):
                            st_l = (kt == kt0)
                            sp_l = (kt == kt1 - 1)
                            pr.on("tensor", lambda e, jj=jj, gi=gi, hp=hp, st_l=st_l, sp_l=sp_l: e.matmul(
                                l_ps[hp][0:1, 0:BLK],
                                ones_col_b[:], pt[jj][:, 512 * gi:512 * (gi + 1)],
                                start=st_l, stop=sp_l), None)
                            pr.on("tensor", lambda e, jj=jj, gi=gi, hp=hp, h=h, kt=kt, kt0=kt0,
                                  sgp=sgp, st_l=st_l, sp_l=sp_l: e.matmul(
                                      o_ps[hp][:, 0:BLK],
                                      vseg[sgp][:].rearrange("p (kv d) -> p kv d", d=D)
                                      [:, kt - kt0, 128 * h:128 * (h + 1)],
                                      pt[jj][:, 512 * gi:512 * (gi + 1)],
                                      start=st_l, stop=sp_l),
                                  st)
                        pt_free[jj] = st.n
                    mmdone = st.n
                    pr.wait("vector", st, mmdone)
                    if si == 0 and m == 0:
                        pass
                    if si == 0:
                        if m == 1:
                            pr.wait("vector", so_osb, osb_stored_all)
                        dr = pr.on("vector", lambda e, h=h, hp=hp: e.tensor_copy(
                            osb[h][:], o_ps[hp][:, 0:BLK]), sv)
                    else:
                        dr = pr.on("vector", lambda e, h=h, hp=hp: e.tensor_tensor(
                            osb[h][:], o_ps[hp][:, 0:BLK], osb[h][:], ALU.add), sv)
                    o_free[hp] = dr
                    pr.wait("vector", so_lacc, lstg_free[hp])
                    dl = pr.on("vector", lambda e, hp=hp: e.tensor_copy(
                        lstg[hp][:], l_ps[hp][0:1, 0:BLK]), sv)
                    l_free[hp] = dl
                    pr.wait("gpsimd", sv, max(dl, lsb_zeroed))
                    pr.wait("gpsimd", so_lacc, so_lacc.n)
                    lstg_free[hp] = pr.on("gpsimd", lambda e, hp=hp, m=m, h=h: e.dma_start(
                        out=lsb[m][h:h + 1, :], in_=lstg[hp][:], accum_op=ALU.add), so_lacc, 16)
                    if si == len(segs) - 1:
                        osb_rdy[h] = dr
                kv_free[sgp] = st.n
            q_free[0] = st.n
            if m == 0:
                for h in range(NH):
                    pr.wait("gpsimd", sv, osb_rdy[h])
                    osb_stored[h] = pr.on("gpsimd", lambda e, h=h: e.dma_start(
                        out=o1_scr[128 * h:128 * (h + 1), :], in_=osb[h][:]), so_osb, 16)
                osb_stored_all = so_osb.n

        pr.wait("vector", so_lacc)
        if debug and qbi == debug_qb:
            pr.wait("gpsimd", sv)
            pr.wait("gpsimd", so_lacc)
            for h in range(NH):
                pr.on("gpsimd", lambda e, h=h: e.dma_start(
                    out=dbg_o2[128 * h:128 * (h + 1), :], in_=osb[h][:]), so_dbg, 16)
            pr.on("gpsimd", lambda e: e.dma_start(out=dbg_l[0:16, :], in_=lsb[0][:]), so_dbg, 16)
            dbgl = pr.on("gpsimd", lambda e: e.dma_start(out=dbg_l[16:32, :], in_=lsb[1][:]), so_dbg, 16)
            pr.wait("vector", so_dbg, dbgl)
        pr.on("vector", lambda e: e.reciprocal(lsb[0][:], lsb[0][:]), sv)
        rec0 = sv.n
        pr.on("vector", lambda e: e.reciprocal(lsb[1][:], lsb[1][:]), sv)
        pr.wait("vector", sv)
        pr.wait("vector", sv, lam16_rdy)
        pr.on("vector", lambda e: e.tensor_scalar(lsb[1][:], lsb[1][:], lam16[:], None, ALU.mult), sv)
        rec_done = sv.n

        msum_mark = {}
        o1h_free = [0, 0]
        bc_free = [0, 0]
        ms_stg_free = 0
        for h in range(NH):
            hp = h % 2
            pr.wait("sync", so_osb, osb_stored_all)
            pr.wait("sync", sv, o1h_free[hp])
            lo1 = pr.on("sync", lambda e, h=h, hp=hp: e.dma_start(
                out=o1h[hp][:], in_=o1_scr[128 * h:128 * (h + 1), :]), ld_o1[hp], 16)
            pr.wait("tensor", sv, max(rec_done, bc_free[0]))
            pr.wait("tensor", sp, egn)
            pr.on("tensor", lambda e, h=h: e.matmul(
                bc_ps[0][:, 0:BLK],
                eoncs[:].rearrange("p (h j) -> p h j", h=16)[:, h, :],
                lsb[0][:], start=True, stop=True), st)
            b1 = st.n
            pr.wait("tensor", sv, bc_free[1])
            pr.on("tensor", lambda e, h=h: e.matmul(
                bc_ps[1][:, 0:BLK],
                eoncs[:].rearrange("p (h j) -> p h j", h=16)[:, h, :],
                lsb[1][:], start=True, stop=True), st)
            b2 = st.n
            pr.wait("vector", st, b1)
            e1 = pr.on("vector", lambda e: e.tensor_copy(bc[0][:], bc_ps[0][:, 0:BLK]), sv)
            pr.wait("vector", st, b2)
            e2 = pr.on("vector", lambda e: e.tensor_copy(bc[1][:], bc_ps[1][:, 0:BLK]), sv)
            pr.wait("vector", ld_o1[hp], lo1)
            pr.wait("vector", sv, e2)
            pr.on("vector", lambda e, hp=hp: e.tensor_tensor(dfn[:], o1h[hp][:], bc[0][:], ALU.mult), sv)
            o1h_free[hp] = sv.n
            bc_free[0] = sv.n
            pr.on("vector", lambda e, h=h: e.tensor_tensor(dfn2[:], osb[h][:], bc[1][:], ALU.mult), sv)
            bc_free[1] = sv.n
            pr.wait("vector", sv)
            pr.on("vector", lambda e: e.tensor_tensor(dfn[:], dfn[:], dfn2[:], ALU.subtract), sv)
            pr.wait("vector", sv)
            pr.on("vector", lambda e, h=h: e.tensor_copy(dff[h][:], dfn[:]), sv)
            pr.on("vector", lambda e: e.tensor_tensor(dfn2[:], dfn[:], dfn[:], ALU.mult), sv)
            d2done = sv.n
            pr.wait("tensor", sv, d2done)
            pr.on("tensor", lambda e: e.matmul(ms_ps[0:1, 0:BLK], ones_col_f[:], dfn2[:],
                                               start=True, stop=True), st)
            msmm = st.n
            pr.wait("vector", st, msmm)
            pr.wait("vector", so_ms, ms_stg_free)
            mscp = pr.on("vector", lambda e: e.tensor_copy(msstg[:], ms_ps[0:1, 0:BLK]), sv)
            pr.wait("tensor", sv, mscp)
            pr.wait("gpsimd", sv, mscp)
            ms_stg_free = pr.on("gpsimd", lambda e, h=h: e.dma_start(
                out=msc[h:h + 1, :], in_=msstg[:]), so_ms, 16)
            msum_mark[h] = ms_stg_free
        pr.wait("scalar", so_ms, msum_mark[NH - 1])
        nln = pr.on("scalar", lambda e: e.activation(
            rstdc[:], msc[:], AF.Ln, scale=1.0 / 128.0, bias=LN_EPS), ss)
        pr.wait("scalar", ss, nln)
        pr.on("scalar", lambda e: e.activation(
            rstdc[:], rstdc[:], AF.Exp, scale=-0.5,
            bias=float(math.log(1.0 - LAMBDA_INIT))), ss)
        rstd_done = ss.n
        dff_rdy = 0
        for h in range(NH):
            pr.wait("tensor", ss, rstd_done)
            pr.wait("tensor", sv, bc_free[0])
            pr.on("tensor", lambda e, h=h: e.matmul(
                bc_ps[0][:, 0:BLK],
                eoncs[:].rearrange("p (h j) -> p h j", h=16)[:, h, :],
                rstdc[:], start=True, stop=True), st)
            b3 = st.n
            pr.wait("vector", st, b3)
            e3 = pr.on("vector", lambda e: e.tensor_copy(bc[0][:], bc_ps[0][:, 0:BLK]), sv)
            pr.wait("vector", sv, e3)
            dff_rdy = pr.on("vector", lambda e, h=h: e.tensor_tensor(
                dff[h][:], dff[h][:], bc[0][:], ALU.mult), sv)
            bc_free[0] = dff_rdy

        if debug and qbi == debug_qb:
            pr.wait("gpsimd", sv, dff_rdy)
            for h in range(NH):
                pr.on("gpsimd", lambda e, h=h: e.dma_start(
                    out=dbg_dff[128 * h:128 * (h + 1), :], in_=dff[h][:]), so_dbg, 16)

        wp_free = 0
        wpf_free = 0
        pj_free = [0, 0]
        xsl_free = [0, 0]
        x1_free = [0, 0]
        for sl in range(D // PSL):
            pr.wait("sync", ss, wpf_free)
            lwp = pr.on("sync", lambda e, sl=sl: e.dma_start(
                out=wpf[:].rearrange("p (a c) -> p a c", a=ND),
                in_=wp_in[:, PSL * sl:PSL * (sl + 1)].rearrange("(a p) c -> p a c", p=128)), ld_wp, 16)
            pr.wait("scalar", ld_wp, lwp)
            pr.wait("scalar", st, wp_free)
            cwp = pr.on("scalar", lambda e: e.copy(wpb[:], wpf[:]), ss)
            wpf_free = cwp
            for qc in range(4):
                pp = qc % 2
                pr.wait("tensor", ss, cwp)
                pr.wait("tensor", sv, max(dff_rdy, pj_free[pp]))
                for h in range(NH):
                    pr.on("tensor", lambda e, h=h, qc=qc, pp=pp: e.matmul(
                        pj_ps[pp][:, 0:PSL],
                        dff[h][:, 128 * qc:128 * (qc + 1)],
                        wpb[:].rearrange("p (a c) -> p a c", a=ND)[:, h, :],
                        start=(h == 0), stop=(h == NH - 1)), st if h == NH - 1 else None)
                pjmm = st.n
                pr.wait("sync", sv, xsl_free[pp])
                lxs = pr.on("sync", lambda e, sl=sl, qc=qc, qbi=qbi, pp=pp: e.dma_start(
                    out=xsl[pp][:], in_=x_in[BLK * qbi + 128 * qc:BLK * qbi + 128 * (qc + 1),
                                             PSL * sl:PSL * (sl + 1)]), ld_xs[pp], 16)
                pr.wait("vector", ld_xs[pp], lxs)
                pr.wait("vector", st, pjmm)
                pr.wait("vector", so_x1[pp], x1_free[pp])
                ev = pr.on("vector", lambda e, pp=pp: e.tensor_tensor(
                    x1st[pp][:], pj_ps[pp][:, 0:PSL], xsl[pp][:], ALU.add), sv)
                pj_free[pp] = ev
                xsl_free[pp] = ev
                pr.wait("gpsimd", sv, ev)
                x1_free[pp] = pr.on("gpsimd", lambda e, sl=sl, qc=qc, qbi=qbi, pp=pp: e.dma_start(
                    out=x1_out[BLK * qbi + 128 * qc:BLK * qbi + 128 * (qc + 1),
                               PSL * sl:PSL * (sl + 1)], in_=x1st[pp][:]), so_x1[pp], 16)
            wp_free = st.n

        pr.wait("tensor", sv)
        pr.wait("tensor", ss)
        for s_ in (so_osb, so_lacc, so_ms, so_x1[0], so_x1[1]):
            pr.wait("tensor", s_)
        o_free[0] = o_free[1] = l_free[0] = l_free[1] = sv.n
        s_free[0] = s_free[1] = 0

    for s_ in (so_osb, so_lacc, so_ms, so_x1[0], so_x1[1], so_dbg):
        pr.wait("gpsimd", s_)
    return pr.emit_and_close()



def build_l0_deq():
    pr = new_prog()
    nc = pr.nc
    gup_pk = nc.dram_tensor("gup_pk", [88 * 128, 32], I32, kind="ExternalInput").ap()
    gup_sc = nc.dram_tensor("gup_sc", [2, 11264], FP16, kind="ExternalInput").ap()
    dn_pk = nc.dram_tensor("dn_pk", [16 * 128, 88], I32, kind="ExternalInput").ap()
    dn_sc = nc.dram_tensor("dn_sc", [11, D], FP16, kind="ExternalInput").ap()
    gupT_out = nc.dram_tensor("gupT", [256, 11264], BF16, kind="ExternalOutput").ap()
    dnT_out = nc.dram_tensor("dnT", [704, D], BF16, kind="ExternalOutput").ap()

    ident = add_ident(pr)
    onesh = pr.sb("onesh", [16, 16], FP16)
    identh = pr.sb("identh", [16, 16], FP16)
    sv = pr.sem("sv")
    ss = pr.sem("ss")
    st = pr.sem("st")
    sg2 = pr.sem("sg2")
    pr.on("vector", lambda e: e.memset(onesh[:], 1.0), sv)
    pr.wait("gpsimd", sv)
    pr.on("gpsimd", lambda e: e.affine_select(
        identh[:], onesh[:], [[-1, 16]], ALU.is_equal, 0.0,
        base=0, channel_multiplier=1), sg2)
    pr.wait("tensor", sg2)

    pk = [pr.sb(f"pk{i}", [128, 88], I32) for i in range(2)]
    LH = pr.sb("LH", [128, 1408], U16)
    BB = pr.sb("BB", [128, 704], U16)
    VV = pr.sb("VV", [128, 704], BF16)
    scg = pr.sb("scg", [2, 11264], FP16)
    scd = pr.sb("scd", [11, D], FP16)
    scgT = pr.sb("scgT", [128, 2], F32)
    scdT = pr.sb("scdT", [128, 11], F32)
    otg = [pr.sb(f"otg{i}", [128, 128], BF16) for i in range(2)]

    tp_ps = pr.ps("tp_ps", [128, 2048], BF16)
    sc_ps = pr.ps("sc_ps", [128, 1024], FP16)

    ld_pk = [pr.sem("ld_pk0"), pr.sem("ld_pk1")]
    ld_sc = pr.sem("ld_sc")
    so_o = [pr.sem("so_o0"), pr.sem("so_o1")]

    pr.on("sync", lambda e: e.dma_start(out=scg[:], in_=gup_sc), ld_sc, 16)
    pr.on("sync", lambda e: e.dma_start(out=scd[:], in_=dn_sc), ld_sc, 16)

    state = {"pk_free": [0, 0], "otg_free": [0, 0], "tp_free": [0, 0],
             "vv_free": 0, "scgT_free": 0, "scdT_free": 0}

    def dequant_tile(part, a, nw, nseg, segw, pk_src, sc_src, scT, ng, out_dst, out_chunks):
        m = a % 2
        pr.wait("sync", sv, state["pk_free"][m])
        lp = pr.on("sync", lambda e, m=m, nw=nw, pk_src=pk_src: e.dma_start(
            out=pk[m][:, 0:nw], in_=pk_src), ld_pk[m], 16)
        pr.wait("tensor", ld_sc, 32)
        pr.wait("tensor", sv, state["scgT_free" if part == "g" else "scdT_free"])
        pr.on("tensor", lambda e, sc_src=sc_src, ng=ng: e.transpose(
            sc_ps[0:128, 0:ng], sc_src, identh[0:ng, 0:ng]), st)
        tsc = st.n
        pr.wait("vector", st, tsc)
        pr.on("vector", lambda e, scT=scT, ng=ng: e.tensor_copy(
            scT[:, 0:ng], sc_ps[0:128, 0:ng]), sv)
        scT_rdy = sv.n
        pr.wait("vector", ld_pk[m], lp)
        pr.wait("vector", st, state["vv_free"])
        u16 = pk[m][:, 0:nw].bitcast(U16)
        nu = 2 * nw
        for k in range(4):
            if 9 - 4 * k >= 0:
                pr.on("vector", lambda e, k=k, nu=nu, u16=u16: e.tensor_scalar(
                    LH[:, nu * k:nu * (k + 1)], u16, 9 - 4 * k, 0x0E00,
                    ALU.logical_shift_left, ALU.bitwise_and), None)
            else:
                pr.on("vector", lambda e, k=k, nu=nu, u16=u16: e.tensor_scalar(
                    LH[:, nu * k:nu * (k + 1)], u16, 4 * k - 9, 0x0E00,
                    ALU.logical_shift_right, ALU.bitwise_and), None)
            pr.on("vector", lambda e, k=k, nu=nu, u16=u16: e.tensor_scalar(
                LH[:, 704 + nu * k:704 + nu * (k + 1)], u16, 12 - 4 * k, 0x8000,
                ALU.logical_shift_left, ALU.bitwise_and),
                sv if k == 3 else None)
        nlh = sv.n
        pr.wait("vector", sv, nlh)
        pr.on("vector", lambda e, nu=nu: e.tensor_tensor(
            BB[:, 0:4 * nu], LH[:, 0:4 * nu], LH[:, 704:704 + 4 * nu], ALU.add), sv)
        badd = sv.n
        pr.wait("vector", sv, max(badd, scT_rdy))
        Bp = BB[:, 0:4 * nu].bitcast(FP16).rearrange("p (k h) -> p k h", k=4).transpose([0, 2, 1])
        for j in range(nseg):
            h0 = (segw * j) // 4
            h1 = (segw * (j + 1)) // 4
            pr.on("vector", lambda e, j=j, h0=h0, h1=h1, Bp=Bp, scT=scT, segw=segw: e.tensor_scalar(
                VV[:, segw * j:segw * (j + 1)],
                Bp[:, h0:h1, :], 16384.0, scT[:, j:j + 1], ALU.mult, ALU.mult),
                sv if j == nseg - 1 else None)
        vready = sv.n
        state["pk_free"][m] = vready
        state["scgT_free" if part == "g" else "scdT_free"] = vready
        pr.wait("tensor", sv, vready)
        for ci, (c0, cw) in enumerate(out_chunks):
            cm = ci % 2
            pr.wait("tensor", sv, state["tp_free"][cm])
            pr.on("tensor", lambda e, c0=c0, cw=cw, cm=cm: e.transpose(
                tp_ps[0:cw, 1024 * cm:1024 * cm + 128], VV[:, c0:c0 + cw], ident[:]), st)
            ttp = st.n
            pr.wait("vector", st, ttp)
            pr.wait("vector", so_o[cm], state["otg_free"][cm])
            ev = pr.on("vector", lambda e, cw=cw, cm=cm: e.tensor_copy(
                otg[cm][0:cw, :], tp_ps[0:cw, 1024 * cm:1024 * cm + 128]), sv)
            state["tp_free"][cm] = ev
            pr.wait("gpsimd", sv, ev)
            state["otg_free"][cm] = pr.on("gpsimd", lambda e, dst=out_dst(c0, cw), cm=cm, cw=cw: e.dma_start(
                out=dst, in_=otg[cm][0:cw, :]), so_o[cm], 16)
        state["vv_free"] = st.n

    for a in range(88):
        dequant_tile(
            "g", a, 32, 2, 128,
            gup_pk[128 * a:128 * (a + 1), :],
            scg[:, 128 * a:128 * (a + 1)],
            scgT, 2,
            lambda c0, cw, a=a: gupT_out[c0:c0 + cw, 128 * a:128 * (a + 1)],
            [(0, 128), (128, 128)])
    for a in range(16):
        dequant_tile(
            "d", a, 88, 11, 64,
            dn_pk[128 * a:128 * (a + 1), :],
            scd[:, 128 * a:128 * (a + 1)],
            scdT, 11,
            lambda c0, cw, a=a: dnT_out[c0:c0 + cw, 128 * a:128 * (a + 1)],
            [(0, 128), (128, 128), (256, 128), (384, 128), (512, 128), (640, 64)])

    for s_ in (so_o[0], so_o[1]):
        pr.wait("gpsimd", s_)
    return pr.emit_and_close()



def build_l0_ffn():
    pr = new_prog()
    nc = pr.nc
    x_in = nc.dram_tensor("x", [TPC, D], F32, kind="ExternalInput").ap()
    gupT_in = nc.dram_tensor("gupT", [D, 11264], BF16, kind="ExternalInput").ap()
    dnT_in = nc.dram_tensor("dnT", [5632, D], BF16, kind="ExternalInput").ap()
    x2_out = nc.dram_tensor("x2", [TPC, D], F32, kind="ExternalOutput").ap()

    ident = add_ident(pr)
    pr.constant(LN_EPS)
    pr.constant(1e-6)
    NI = 44
    DSL = 256

    xt = [pr.sb(f"xt{a}", [128, BLK], BF16) for a in range(ND)]
    xbuf = [pr.sb(f"xbuf{i}", [128, D], F32) for i in range(2)]
    xh = [pr.sb(f"xh{i}", [128, D], BF16) for i in range(2)]
    sqd = pr.sb("sqd", [128, D], F32)
    stat = [pr.sb(f"stat{i}", [128, 8], F32) for i in range(NT)]
    wg = [pr.sb(f"wg{i}", [128, ND * 128], BF16) for i in range(2)]
    wu = [pr.sb(f"wu{i}", [128, ND * 128], BF16) for i in range(2)]
    hsb = [pr.sb(f"hsb{i}", [128, BLK], BF16) for i in range(NI)]
    gs = [pr.sb(f"gs{i}", [128, BLK], BF16) for i in range(2)]
    dsl = pr.sb("dsl", [128, NI * DSL], BF16)
    ysb = [pr.sb(f"ysb{i}", [128, D], F32) for i in range(4)]
    yssq = pr.sb("yssq", [128, 4], F32)
    yrstd = pr.sb("yrstd", [128, 4], F32)
    xr = [pr.sb(f"xr{i}", [128, D], F32) for i in range(2)]
    x2st = [pr.sb(f"x2st{i}", [128, D], F32) for i in range(2)]

    tp_ps = [pr.ps(f"tp{i}", [128, 8 * 128], BF16) for i in range(2)]
    gu_ps = pr.ps("gu_ps", [128, 2048])
    g_ps = [gu_ps[:, 0:BLK], gu_ps[:, BLK:1024]]
    u_ps = [gu_ps[:, 1024:1536], gu_ps[:, 1536:2048]]
    dn_ps2 = pr.ps("dn_ps2", [128, 1024])
    dn_ps = [dn_ps2[:, 0:BLK], dn_ps2[:, BLK:1024]]

    ldx = [pr.sem("ldx0"), pr.sem("ldx1")]
    ldg = [pr.sem("ldg0"), pr.sem("ldg1")]
    ldu = [pr.sem("ldu0"), pr.sem("ldu1")]
    ldd = pr.sem("ldd")
    ldr = [pr.sem("ldr0"), pr.sem("ldr1")]
    sv = pr.sem("sv")
    ss = pr.sem("ss")
    st = pr.sem("st")
    so2 = [pr.sem("so20"), pr.sem("so21")]

    x_free = [0, 0]
    xh_free = [0, 0]
    tp_done = {}
    xr_free = [0, 0]
    x2_free = [0, 0]
    wg_free = [0, 0]
    wu_free = [0, 0]
    gps_free = [0, 0]
    ups_free = [0, 0]
    gs_free = [0, 0]
    dn_free = [0, 0]
    dsl_free = 0
    hsb_last = 0

    for hf in range(2):
        for i4 in range(4):
            i = 4 * hf + i4
            m = i % 2
            pr.wait("sync", sv, x_free[m])
            lx = pr.on("sync", lambda e, i=i, m=m: e.dma_start(
                out=xbuf[m][:], in_=x_in[128 * i:128 * (i + 1), :]), ldx[m], 16)
            pr.wait("vector", ldx[m], lx)
            nred = pr.on("vector", lambda e, m=m, i=i: e.reduce_sum(
                stat[i][:, 0:1], xbuf[m][:], axis=AX.X), sv)
            pr.wait("vector", sv, nred)
            pr.on("vector", lambda e, i=i: e.tensor_scalar(
                stat[i][:, 2:3], stat[i][:, 0:1], -1.0 / D, None, ALU.mult), sv)
            pr.wait("scalar", ldx[m], lx)
            pr.on("scalar", lambda e, m=m, i=i: e.activation(
                sqd[:], xbuf[m][:], AF.Square, scale=1.0 / math.sqrt(D),
                accum_out=stat[i][:, 1:2]))
            pr.wait("scalar", sv)
            pr.on("scalar", lambda e, i=i: e.activation(
                stat[i][:, 3:4], stat[i][:, 2:3], AF.Square), ss)
            pr.wait("vector", ss)
            pr.on("vector", lambda e, i=i: e.tensor_tensor(
                stat[i][:, 4:5], stat[i][:, 1:2], stat[i][:, 3:4], ALU.subtract), sv)
            pr.wait("scalar", sv)
            nln = pr.on("scalar", lambda e, i=i: e.activation(
                stat[i][:, 5:6], stat[i][:, 4:5], AF.Ln, bias=LN_EPS), ss)
            pr.wait("scalar", ss, nln)
            pr.on("scalar", lambda e, i=i: e.activation(
                stat[i][:, 5:6], stat[i][:, 5:6], AF.Exp, scale=-0.5), ss)
            pr.wait("vector", ss)
            pr.wait("vector", st, xh_free[m])
            xhn = pr.on("vector", lambda e, m=m, i=i: e.tensor_scalar(
                xh[m][:], xbuf[m][:], stat[i][:, 2:3], stat[i][:, 5:6], ALU.add, ALU.mult), sv)
            x_free[m] = xhn
            pr.wait("tensor", sv, xhn)
            for grp in range(2):
                pr.wait("tensor", sv, tp_done.get((i, grp - 1) if grp else (i - 1, 1), 0))
                for a8 in range(8):
                    a = grp * 8 + a8
                    pr.on("tensor", lambda e, m=m, a=a, a8=a8, grp=grp: e.transpose(
                        tp_ps[grp][:, 128 * a8:128 * (a8 + 1)],
                        xh[m][:, 128 * a:128 * (a + 1)], ident[:]),
                        st if a8 == 7 else None)
                tg = st.n
                pr.wait("vector", st, tg)
                for a8 in range(8):
                    a = grp * 8 + a8
                    pr.on("vector", lambda e, i4=i4, a=a, a8=a8, grp=grp: e.tensor_copy(
                        xt[a][:, 128 * i4:128 * (i4 + 1)],
                        tp_ps[grp][:, 128 * a8:128 * (a8 + 1)]),
                        sv if a8 == 7 else None)
                tp_done[(i, grp)] = sv.n
            xh_free[m] = st.n
        xt_rdy = sv.n

        for ic in range(NI):
            m = ic % 2
            pr.wait("sync", st, wg_free[m])
            lg = pr.on("sync", lambda e, ic=ic, m=m: e.dma_start(
                out=wg[m][:].rearrange("p (a c) -> p a c", a=ND),
                in_=gupT_in[:, 128 * ic:128 * (ic + 1)].rearrange("(a p) c -> p a c", p=128)),
                ldg[m], 16)
            pr.wait("sync", st, wu_free[m])
            lu = pr.on("sync", lambda e, ic=ic, m=m: e.dma_start(
                out=wu[m][:].rearrange("p (a c) -> p a c", a=ND),
                in_=gupT_in[:, 5632 + 128 * ic:5632 + 128 * (ic + 1)].rearrange("(a p) c -> p a c", p=128)),
                ldu[m], 16)
            pr.wait("tensor", sv, xt_rdy)
            pr.wait("tensor", ldg[m], lg)
            pr.wait("tensor", ss, gps_free[m])
            for a in range(ND):
                pr.on("tensor", lambda e, a=a, m=m: e.matmul(
                    g_ps[m][:, 0:BLK],
                    wg[m][:].rearrange("p (a c) -> p a c", a=ND)[:, a, :],
                    xt[a][:], start=(a == 0), stop=(a == ND - 1)),
                    st if a == ND - 1 else None)
            gmm = st.n
            pr.wait("tensor", ldu[m], lu)
            pr.wait("tensor", sv, ups_free[m])
            for a in range(ND):
                pr.on("tensor", lambda e, a=a, m=m: e.matmul(
                    u_ps[m][:, 0:BLK],
                    wu[m][:].rearrange("p (a c) -> p a c", a=ND)[:, a, :],
                    xt[a][:], start=(a == 0), stop=(a == ND - 1)),
                    st if a == ND - 1 else None)
            umm = st.n
            wg_free[m] = gmm
            wu_free[m] = umm
            pr.wait("scalar", st, gmm)
            pr.wait("scalar", sv, gs_free[m])
            sl = pr.on("scalar", lambda e, m=m: e.activation(
                gs[m][:], g_ps[m][:, 0:BLK], AF.Silu), ss)
            gps_free[m] = sl
            pr.wait("vector", ss, sl)
            pr.wait("vector", st, umm)
            hm = pr.on("vector", lambda e, ic=ic, m=m: e.tensor_tensor(
                hsb[ic][:], gs[m][:], u_ps[m][:, 0:BLK], ALU.mult), sv)
            ups_free[m] = hm
            gs_free[m] = hm
        hsb_last = sv.n

        for dc in range(D // DSL):
            pr.wait("sync", st, dsl_free)
            ldn = pr.on("sync", lambda e, dc=dc: e.dma_start(
                out=dsl[:].rearrange("p (a c) -> p a c", a=NI),
                in_=dnT_in[:, DSL * dc:DSL * (dc + 1)].rearrange("(a p) c -> p a c", p=128)),
                ldd, 16)
            for t2 in range(4):
                pp = (4 * dc + t2) % 2
                pr.wait("tensor", ldd, ldn)
                pr.wait("tensor", sv, max(hsb_last, dn_free[pp]))
                for icc in range(NI):
                    pr.on("tensor", lambda e, icc=icc, t2=t2, pp=pp: e.matmul(
                        dn_ps[pp][:, 0:DSL],
                        hsb[icc][:, 128 * t2:128 * (t2 + 1)],
                        dsl[:].rearrange("p (a c) -> p a c", a=NI)[:, icc, :],
                        start=(icc == 0), stop=(icc == NI - 1)),
                        st if icc == NI - 1 else None)
                dmm = st.n
                pr.wait("vector", st, dmm)
                dn_free[pp] = pr.on("vector", lambda e, t2=t2, dc=dc, pp=pp: e.tensor_copy(
                    ysb[t2][:, DSL * dc:DSL * (dc + 1)], dn_ps[pp][:, 0:DSL]), sv)
            dsl_free = st.n
        ydone = sv.n

        for t2 in range(4):
            m = t2 % 2
            pr.wait("scalar", sv, ydone)
            pr.on("scalar", lambda e, t2=t2: e.activation(
                sqd[:], ysb[t2][:], AF.Square, scale=1.0 / math.sqrt(D),
                accum_out=yssq[:, t2:t2 + 1]), ss)
            nq = ss.n
            pr.wait("scalar", ss, nq)
            nln = pr.on("scalar", lambda e, t2=t2: e.activation(
                yrstd[:, t2:t2 + 1], yssq[:, t2:t2 + 1], AF.Ln, bias=1e-6), ss)
            pr.wait("scalar", ss, nln)
            pr.on("scalar", lambda e, t2=t2: e.activation(
                yrstd[:, t2:t2 + 1], yrstd[:, t2:t2 + 1], AF.Exp, scale=-0.5), ss)
            rdn = ss.n
            pr.wait("sync", sv, xr_free[m])
            lxr = pr.on("sync", lambda e, hf=hf, t2=t2, m=m: e.dma_start(
                out=xr[m][:], in_=x_in[BLK * hf + 128 * t2:BLK * hf + 128 * (t2 + 1), :]),
                ldr[m], 16)
            pr.wait("vector", ldr[m], lxr)
            pr.wait("vector", ss, rdn)
            pr.wait("vector", so2[m], x2_free[m])
            ev = pr.on("vector", lambda e, t2=t2, m=m: e.scalar_tensor_tensor(
                x2st[m][:], ysb[t2][:], yrstd[:, t2:t2 + 1], xr[m][:],
                ALU.mult, ALU.add), sv)
            xr_free[m] = ev
            pr.wait("gpsimd", sv, ev)
            x2_free[m] = pr.on("gpsimd", lambda e, hf=hf, t2=t2, m=m: e.dma_start(
                out=x2_out[BLK * hf + 128 * t2:BLK * hf + 128 * (t2 + 1), :],
                in_=x2st[m][:]), so2[m], 16)
        pr.wait("tensor", sv)
        pr.wait("vector", st)
        xt_rdy = 0

    for s_ in (so2[0], so2[1]):
        pr.wait("gpsimd", s_)
    return pr.emit_and_close()



F32_EPS = 1.1920929e-07


def build_l1_qkv():
    pr = new_prog()
    nc = pr.nc
    x_in = nc.dram_tensor("x", [TPC, D], F32, kind="ExternalInput").ap()
    wn_in = nc.dram_tensor("wn", [16, 128], F32, kind="ExternalInput").ap()
    w_in = nc.dram_tensor("wqkv", [D, 3 * D], F32, kind="ExternalInput").ap()
    qf_out = nc.dram_tensor("qf", [D, TPC], BF16, kind="ExternalOutput").ap()
    kf_out = nc.dram_tensor("kf", [D, TPC], BF16, kind="ExternalOutput").ap()
    v_out = nc.dram_tensor("v", [TPC, D], BF16, kind="ExternalOutput").ap()
    pid_ap = nc.partition_id_tensor.ap()

    ident = add_ident(pr)
    pr.constant(F32_EPS)
    pr.constant(LN_EPS)
    pr.constant(math.pi / 2)
    onesf = pr.sb("onesf", [16, 16], F32)
    identf = pr.sb("identf", [16, 16], F32)
    sv = pr.sem("sv")
    ss = pr.sem("ss")
    st = pr.sem("st")
    sg2 = pr.sem("sg2")
    pr.on("vector", lambda e: e.memset(onesf[:], 1.0), sv)
    pr.wait("gpsimd", sv)
    pr.on("gpsimd", lambda e: e.affine_select(
        identf[:], onesf[:], [[-1, 16]], ALU.is_equal, 0.0,
        base=0, channel_multiplier=1), sg2)
    pr.wait("tensor", sg2)

    xt = [pr.sb(f"xt{a}", [128, TPC], BF16) for a in range(ND)]
    xbuf = [pr.sb(f"xbuf{i}", [128, D], F32) for i in range(2)]
    xh = [pr.sb(f"xh{i}", [128, D], BF16) for i in range(2)]
    sqd = pr.sb("sqd", [128, D], F32)
    stat = [pr.sb(f"stat{i}", [128, 8], F32) for i in range(NT)]
    wf = [pr.sb(f"wf{i}", [128, ND * 256], F32) for i in range(2)]
    wb = [pr.sb(f"wb{i}", [128, ND * 256], BF16) for i in range(2)]
    wn_rows = pr.sb("wn_rows", [16, 128], F32)
    wnT = pr.sb("wnT", [128, 16], F32)
    pid_u = pr.sb("pid_u", [1, 1], mybir.dt.uint32)
    pidf = pr.sb("pidf", [1, 4], F32)
    ones_row1 = pr.sb("ones_row1", [1, 128], F32)
    pid128 = pr.sb("pid128", [128, 2], F32)
    iop = pr.sb("iop", [128, 1], F32)
    pmod = pr.sb("pmod", [128, 1], F32)
    pairi = pr.sb("pairi", [128, 1], F32)
    invf = pr.sb("invf", [128, 1], F32)
    sigp = pr.sb("sigp", [128, 1], F32)
    iof = pr.sb("iof", [128, BLK], F32)
    tg = pr.sb("tg", [128, TPC], F32)
    ang = pr.sb("ang", [128, TPC], F32)
    cos_t = pr.sb("cos_t", [128, TPC], F32)
    sins = pr.sb("sins", [128, TPC], F32)
    qtmp = pr.sb("qtmp", [128, TPC], F32)
    qsh = pr.sb("qsh", [128, TPC], F32)
    qc2 = pr.sb("qc2", [128, TPC], F32)
    ostg = [pr.sb(f"ostg{i}", [128, TPC], BF16) for i in range(2)]
    vstg = [pr.sb(f"vstg{i}", [128, 256], BF16) for i in range(2)]

    tp_ps = [pr.ps(f"tp{i}", [128, 8 * 128], BF16) for i in range(2)]
    qk_ps = [pr.ps(f"qk{i}", [128, TPC]) for i in range(2)]
    v_ps = [pr.ps(f"vps{i}", [128, 256]) for i in range(2)]
    sc_ps = qk_ps[0]

    ldx = [pr.sem("ldx0"), pr.sem("ldx1")]
    ldw = [pr.sem("ldw0"), pr.sem("ldw1")]
    ldm = pr.sem("ldm")
    soq = [pr.sem("soq0"), pr.sem("soq1")]
    sov = [pr.sem("sov0"), pr.sem("sov1")]

    PI = math.pi

    lm = pr.on("sync", lambda e: e.dma_start(out=wn_rows[:], in_=wn_in), ldm, 16)
    lm = pr.on("sync", lambda e: e.dma_start(out=pid_u[:], in_=pid_ap), ldm, 16)
    pr.wait("tensor", ldm, 32)
    pr.on("tensor", lambda e: e.transpose(sc_ps[0:128, 0:16], wn_rows[:], identf[:]), st)
    twn = st.n
    pr.wait("vector", st, twn)
    pr.on("vector", lambda e: e.tensor_copy(wnT[:], sc_ps[0:128, 0:16]), sv)
    wn_rdy = sv.n
    pr.wait("vector", ldm, 32)
    pr.on("vector", lambda e: e.tensor_copy(pidf[:, 0:1], pid_u[:]), sv)
    pr.wait("vector", sv)
    pr.on("vector", lambda e: e.tensor_scalar(pidf[:, 1:2], pidf[:, 0:1], 512.0, None, ALU.mult), sv)
    pr.wait("vector", sv)
    pr.on("vector", lambda e: e.tensor_scalar(pidf[:, 2:3], pidf[:, 1:2], -1.0, 512.0 * 15, ALU.mult, ALU.add), sv)
    pr.on("vector", lambda e: e.memset(ones_row1[:], 1.0), sv)
    pid_rdy = sv.n
    pr.wait("tensor", sv, pid_rdy)
    pr.on("tensor", lambda e: e.matmul(sc_ps[:, 16:17], ones_row1[:], pidf[0:1, 1:2],
                                       start=True, stop=True), None)
    pr.on("tensor", lambda e: e.matmul(sc_ps[:, 17:18], ones_row1[:], pidf[0:1, 2:3],
                                       start=True, stop=True), st)
    tpid = st.n
    pr.wait("vector", st, tpid)
    pr.on("vector", lambda e: e.tensor_copy(pid128[:], sc_ps[:, 16:18]), sv)
    pr.wait("gpsimd", sv)
    pr.on("gpsimd", lambda e: e.iota(iop[:], [[0, 1]], base=0, channel_multiplier=1,
                                     allow_small_or_imprecise_dtypes=True), sp_ := None) if False else None
    sgio = pr.sem("sgio")
    iopi = pr.sb("iopi", [128, 1], I32)
    pmodi = pr.sb("pmodi", [128, 1], I32)
    angi = pr.sb("angi", [128, TPC], I32)
    pr.on("gpsimd", lambda e: e.iota(iopi[:], [[0, 1]], base=0, channel_multiplier=1), sgio)
    pr.on("gpsimd", lambda e: e.iota(iof[:], [[1, BLK]], base=0, channel_multiplier=0,
                                     allow_small_or_imprecise_dtypes=True), sgio)
    pr.wait("vector", sgio, sgio.n)
    pr.on("vector", lambda e: e.tensor_scalar(pmodi[:], iopi[:], 1, None, ALU.bitwise_and), sv)
    pr.wait("vector", sv)
    pr.on("vector", lambda e: e.tensor_tensor(iopi[:], iopi[:], pmodi[:], ALU.subtract), sv)
    pr.on("vector", lambda e: e.tensor_copy(sigp[:], pmodi[:]), sv)
    pr.wait("vector", sv)
    pr.on("vector", lambda e: e.tensor_copy(pairi[:], iopi[:]), sv)
    pr.on("vector", lambda e: e.tensor_scalar(sigp[:], sigp[:], 2.0, -1.0, ALU.mult, ALU.add), sv)
    pr.wait("vector", sv)
    pr.on("vector", lambda e: e.tensor_scalar(pairi[:], pairi[:], 0.5, None, ALU.mult), sv)
    pr.wait("scalar", sv)
    pr.on("scalar", lambda e: e.activation(invf[:], pairi[:], AF.Exp,
                                           scale=-math.log(10000.0) / 64.0), ss)
    pr.wait("vector", ss)
    pr.on("vector", lambda e: e.tensor_scalar(tg[:, 0:BLK], iof[:], pid128[:, 0:1], None, ALU.add), sv)
    pr.on("vector", lambda e: e.tensor_scalar(tg[:, BLK:TPC], iof[:], pid128[:, 1:2], None, ALU.add), sv)
    pr.wait("vector", sv)
    pr.on("vector", lambda e: e.tensor_scalar(ang[:], tg[:], invf[:], None, ALU.mult), sv)
    pr.wait("vector", sv)
    pr.on("vector", lambda e: e.tensor_scalar(qtmp[:], ang[:], 1.0 / (2 * PI), None, ALU.mult), sv)
    pr.wait("vector", sv)
    pr.on("vector", lambda e: e.tensor_copy(angi[:], qtmp[:]), sv)
    pr.wait("vector", sv)
    pr.on("vector", lambda e: e.tensor_copy(qtmp[:], angi[:]), sv)
    pr.wait("vector", sv)
    pr.on("vector", lambda e: e.scalar_tensor_tensor(
        qtmp[:], qtmp[:], -2 * PI, ang[:], ALU.mult, ALU.add), sv)
    pr.wait("scalar", sv)
    pr.on("scalar", lambda e: e.activation(sins[:], qtmp[:], AF.Sin), ss)
    pr.on("vector", lambda e: e.tensor_scalar(qsh[:], ang[:], 1.0 / (2 * PI), 0.25, ALU.mult, ALU.add), sv)
    pr.wait("vector", sv)
    pr.on("vector", lambda e: e.tensor_copy(angi[:], qsh[:]), sv)
    pr.wait("vector", sv)
    pr.on("vector", lambda e: e.tensor_copy(qsh[:], angi[:]), sv)
    pr.wait("vector", sv)
    pr.on("vector", lambda e: e.scalar_tensor_tensor(
        qsh[:], qsh[:], -2 * PI, ang[:], ALU.mult, ALU.add), sv)
    pr.wait("scalar", sv)
    pr.on("scalar", lambda e: e.activation(cos_t[:], qsh[:], AF.Sin, bias=PI / 2), ss)
    pr.wait("vector", ss)
    pr.on("vector", lambda e: e.tensor_scalar(sins[:], sins[:], sigp[:], None, ALU.mult), sv)
    rope_rdy = sv.n

    x_free = [0, 0]
    xh_free = [0, 0]
    tp_done = {}
    for i in range(NT):
        m = i % 2
        pr.wait("sync", ss, x_free[m])
        lx = pr.on("sync", lambda e, i=i, m=m: e.dma_start(
            out=xbuf[m][:], in_=x_in[128 * i:128 * (i + 1), :]), ldx[m], 16)
        pr.wait("scalar", ldx[m], lx)
        n1 = pr.on("scalar", lambda e, m=m, i=i: e.activation(
            sqd[:], xbuf[m][:], AF.Square, scale=1.0 / math.sqrt(D),
            accum_out=stat[i][:, 0:1]), ss)
        pr.wait("scalar", ss, n1)
        n2 = pr.on("scalar", lambda e, m=m, i=i: e.activation(
            sqd[:], xbuf[m][:], AF.Identity, scale=1.0 / D,
            accum_out=stat[i][:, 1:2]), ss)
        pr.wait("vector", ss, n2)
        pr.on("vector", lambda e, i=i: e.tensor_tensor(
            stat[i][:, 2:3], stat[i][:, 1:2], stat[i][:, 1:2], ALU.mult), sv)
        n3 = sv.n
        pr.wait("vector", sv, n3)
        n4 = pr.on("vector", lambda e, i=i: e.tensor_tensor(
            stat[i][:, 2:3], stat[i][:, 0:1], stat[i][:, 2:3], ALU.subtract), sv)
        pr.wait("scalar", sv, n4)
        n5 = pr.on("scalar", lambda e, i=i: e.activation(
            stat[i][:, 3:4], stat[i][:, 2:3], AF.Ln, bias=LN_EPS), ss)
        pr.wait("scalar", ss, n5)
        pr.on("scalar", lambda e, i=i: e.activation(
            stat[i][:, 4:5], stat[i][:, 3:4], AF.Exp, scale=-0.5), ss)
        n7 = pr.on("scalar", lambda e, i=i: e.activation(
            stat[i][:, 5:6], stat[i][:, 3:4], AF.Exp, scale=-1.0), ss)
        pr.wait("vector", ss, n7)
        n8 = pr.on("vector", lambda e, i=i: e.tensor_tensor(
            stat[i][:, 5:6], stat[i][:, 2:3], stat[i][:, 5:6], ALU.mult), sv)
        pr.wait("scalar", sv, n8)
        n9 = pr.on("scalar", lambda e, i=i: e.activation(
            stat[i][:, 5:6], stat[i][:, 5:6], AF.Ln, bias=F32_EPS), ss)
        pr.wait("scalar", ss, n9)
        n10 = pr.on("scalar", lambda e, i=i: e.activation(
            stat[i][:, 5:6], stat[i][:, 5:6], AF.Exp, scale=-0.5), ss)
        pr.wait("vector", ss, n10)
        n11 = pr.on("vector", lambda e, i=i: e.tensor_tensor(
            stat[i][:, 6:7], stat[i][:, 4:5], stat[i][:, 5:6], ALU.mult), sv)
        pr.wait("vector", sv, n11)
        n12 = pr.on("vector", lambda e, i=i: e.scalar_tensor_tensor(
            stat[i][:, 7:8], stat[i][:, 1:2], -1.0, stat[i][:, 6:7],
            ALU.mult, ALU.mult), sv)
        pr.wait("scalar", sv, n12)
        pr.wait("scalar", st, xh_free[m])
        xhn = pr.on("scalar", lambda e, m=m, i=i: e.activation(
            xh[m][:], xbuf[m][:], AF.Identity,
            scale=stat[i][:, 6:7], bias=stat[i][:, 7:8]), ss)
        x_free[m] = xhn
        pr.wait("tensor", ss, xhn)
        for grp in range(2):
            pr.wait("tensor", sv, tp_done.get((i, grp - 1) if grp else (i - 1, 1), 0))
            for a8 in range(8):
                a = grp * 8 + a8
                pr.on("tensor", lambda e, m=m, a=a, a8=a8, grp=grp: e.transpose(
                    tp_ps[grp][:, 128 * a8:128 * (a8 + 1)],
                    xh[m][:, 128 * a:128 * (a + 1)], ident[:]),
                    st if a8 == 7 else None)
            tg2 = st.n
            pr.wait("vector", st, tg2)
            for a8 in range(8):
                a = grp * 8 + a8
                pr.on("vector", lambda e, i=i, a=a, a8=a8, grp=grp: e.tensor_copy(
                    xt[a][:, 128 * i:128 * (i + 1)],
                    tp_ps[grp][:, 128 * a8:128 * (a8 + 1)]),
                    sv if a8 == 7 else None)
            tp_done[(i, grp)] = sv.n
        xh_free[m] = st.n
    xt_done = sv.n

    SW = 256
    NSLAB = 24
    ostg_free = [0, 0]
    vstg_free = [0, 0]
    w_free = [0, 0]
    wf_free = [0, 0]
    qkev = {}
    vev = {}
    rope_busy = 0
    pr.wait("tensor", sv, xt_done)
    for s in range(NSLAB):
        m = s % 2
        pr.wait("sync", ss, wf_free[m])
        lw = pr.on("sync", lambda e, s=s, m=m: e.dma_start(
            out=wf[m][:].rearrange("p (a o) -> p a o", a=ND),
            in_=w_in[:, SW * s:SW * (s + 1)].rearrange("(a p) o -> p a o", p=128)), ldw[m], 16)
        pr.wait("scalar", ldw[m], lw)
        pr.wait("scalar", st, w_free[m])
        pr.wait("scalar", sv, wn_rdy)
        for a in range(ND):
            pr.on("scalar", lambda e, m=m, a=a: e.activation(
                wb[m][:].rearrange("p (a o) -> p a o", a=ND)[:, a, :],
                wf[m][:].rearrange("p (a o) -> p a o", a=ND)[:, a, :],
                AF.Identity, scale=wnT[:, a:a + 1]),
                ss if a == ND - 1 else None)
        cw = ss.n
        wf_free[m] = cw
        pr.wait("tensor", ss, cw)
        wbv = wb[m][:].rearrange("p (a o) -> p a o", a=ND)
        if s < 16:
            for oc in range(2):
                j = 2 * s + oc
                jm = j % 2
                pr.wait("tensor", sv, qkev.get(j - 2, 0))
                for a in range(ND):
                    for hh in range(2):
                        pr.on("tensor", lambda e, a=a, oc=oc, hh=hh, jm=jm, wbv=wbv: e.matmul(
                            qk_ps[jm][:, BLK * hh:BLK * (hh + 1)],
                            wbv[:, a, 128 * oc:128 * (oc + 1)],
                            xt[a][:, BLK * hh:BLK * (hh + 1)],
                            start=(a == 0), stop=(a == ND - 1)),
                            st if (a == ND - 1 and hh == 1) else None)
                tq = st.n
                pr.wait("vector", st, tq)
                pr.wait("vector", sv, max(rope_rdy, rope_busy))
                pr.on("vector", lambda e, jm=jm: e.tensor_copy(qtmp[:], qk_ps[jm][:]), sv)
                evq = sv.n
                pr.wait("vector", sv, evq)
                perm = []
                for ii in range(0, 32, 2):
                    perm += [ii + 1, ii]
                pr.on("vector", lambda e, perm=perm: e.stream_shuffle(qsh[:], qtmp[:], perm), sv)
                pr.on("vector", lambda e: e.tensor_tensor(qc2[:], qtmp[:], cos_t[:], ALU.mult), sv)
                pr.wait("vector", sv)
                pr.on("vector", lambda e: e.tensor_tensor(qsh[:], qsh[:], sins[:], ALU.mult), sv)
                pr.wait("vector", sv)
                pr.wait("vector", soq[jm], ostg_free[jm])
                ev = pr.on("vector", lambda e, jm=jm: e.tensor_tensor(
                    ostg[jm][:], qc2[:], qsh[:], ALU.add), sv)
                rope_busy = ev
                qkev[j] = ev
                if j < 16:
                    dst = qf_out[128 * j:128 * (j + 1), :]
                else:
                    dst = kf_out[128 * (j - 16):128 * (j - 16) + 128, :]
                pr.wait("gpsimd", sv, ev)
                ostg_free[jm] = pr.on("gpsimd", lambda e, dst=dst, jm=jm: e.dma_start(
                    out=dst, in_=ostg[jm][:]), soq[jm], 16)
        else:
            vs = s - 16
            for t in range(NT):
                jm = (vs * NT + t) % 2
                pr.wait("tensor", sv, vev.get((vs * NT + t) - 2, 0))
                for a in range(ND):
                    pr.on("tensor", lambda e, a=a, t=t, jm=jm, wbv=wbv: e.matmul(
                        v_ps[jm][:, 0:SW], xt[a][:, 128 * t:128 * (t + 1)],
                        wbv[:, a, :],
                        start=(a == 0), stop=(a == ND - 1)),
                        st if a == ND - 1 else None)
                tv = st.n
                pr.wait("vector", st, tv)
                pr.wait("vector", sov[jm], vstg_free[jm])
                ev = pr.on("vector", lambda e, jm=jm: e.tensor_copy(
                    vstg[jm][:, 0:SW], v_ps[jm][:, 0:SW]), sv)
                vev[vs * NT + t] = ev
                pr.wait("gpsimd", sv, ev)
                vstg_free[jm] = pr.on("gpsimd", lambda e, t=t, vs=vs, jm=jm: e.dma_start(
                    out=v_out[128 * t:128 * (t + 1), SW * vs:SW * (vs + 1)],
                    in_=vstg[jm][:, 0:SW]), sov[jm], 16)
        w_free[m] = st.n

    for s_ in (soq[0], soq[1], sov[0], sov[1]):
        pr.wait("gpsimd", s_)
    return pr.emit_and_close()


def build_l1_attn(core, debug=False):
    pr = new_prog()
    nc = pr.nc
    qf_in = nc.dram_tensor("qf", [D, TPC], BF16, kind="ExternalInput").ap()
    kf_in = nc.dram_tensor("kf", [D, SEQ], BF16, kind="ExternalInput").ap()
    v_in = nc.dram_tensor("v", [SEQ, D], BF16, kind="ExternalInput").ap()
    pm_in = nc.dram_tensor("pm", [16, 2 * 16 * 128], F32, kind="ExternalInput").ap()
    wo_in = nc.dram_tensor("wout", [D, D], F32, kind="ExternalInput").ap()
    x_in = nc.dram_tensor("x", [TPC, D], F32, kind="ExternalInput").ap()
    x3_out = nc.dram_tensor("x3", [TPC, D], F32, kind="ExternalOutput").ap()
    if debug:
        dbg_xo = nc.dram_tensor("dbg_xo", [16 * 128, BLK], BF16, kind="ExternalOutput").ap()
        so_dbg = pr.sem("so_dbg")

    ident = add_ident(pr)
    pr.constant(1e-12)
    NH = 16
    PSL = 256
    SCL = 1.0 / math.sqrt(128.0)
    qsb = pr.sb("qsb", [128, NH * BLK], BF16)
    kw = pr.sb("kw", [128, NH * 1024], BF16)
    vw = pr.sb("vw", [128, 8 * D], BF16)
    pmf = pr.sb("pmf", [16, 2 * NH * 128], F32)
    pmb = pr.sb("pmb", [16, 2 * NH * 128], BF16)
    kpmT = pr.sb("kpmT", [128, NH * 16], BF16)
    pt = [pr.sb(f"pt{i}", [128, 1024], BF16) for i in range(2)]
    ptm = pr.sb("ptm", [16, BLK], BF16)
    sqk = [pr.sb(f"sqk{i}", [128, BLK], F32) for i in range(2)]
    sqp = pr.sb("sqp", [16, 2 * NH * 128], F32)
    pmn = pr.sb("pmn", [16, 16], F32)
    pmn1 = pr.sb("pmn1", [16, 1], F32)
    mkrow = pr.sb("mkrow", [1, 34], F32)
    crow = pr.sb("crow", [1, BLK], F32)
    lstg = [pr.sb(f"lstg{i}", [1, BLK], F32) for i in range(2)]
    bc0 = pr.sb("bc0", [128, BLK], F32)
    xo = [pr.sb(f"xo{h}", [128, BLK], BF16) for h in range(NH)]
    onesf = pr.sb("onesf_", [16, 16], F32)
    identf = pr.sb("identf_", [16, 16], F32)
    ones_col_b = pr.sb("ones_col_b", [128, 1], BF16)
    ones_col_f = pr.sb("ones_col_f", [128, 1], F32)
    ones_row1 = pr.sb("ones_row1", [1, 128], F32)
    negor = pr.sb("negor", [1, 128], F32)
    wpf = pr.sb("wpf", [128, ND * PSL], F32)
    wpb = pr.sb("wpb", [128, ND * PSL], BF16)
    xsl = [pr.sb(f"xsl{i}", [128, PSL], F32) for i in range(2)]
    x3st = [pr.sb(f"x3st{i}", [128, PSL], F32) for i in range(2)]

    A = pr.ps("arenaA", [128, 2048])
    B = pr.ps("arenaB", [128, 2048])
    s_ps = [A[:, 0:1024], A[:, 1024:2048]]
    o_ps = [B[:, 0:BLK], B[:, BLK:1024]]
    l_ps = B[0:1, 1024:1536]
    nm_ps = [B[32:33, 1024:1536], B[64:65, 1024:1536]]
    pmbank = B[:, 1536:2048]
    pj_ps = [B[:, 0:BLK], B[:, BLK:1024]]
    tsp = [A[:, 0:8].bitcast(BF16), A[:, 8:16].bitcast(BF16)]

    ld_q = pr.sem("ld_q")
    ld_k = pr.sem("ld_k")
    ld_v = pr.sem("ld_v")
    ld_pm = pr.sem("ld_pm")
    ld_xs = [pr.sem("ld_xs0"), pr.sem("ld_xs1")]
    ld_wp = pr.sem("ld_wp")
    sv = pr.sem("sv")
    ss = pr.sem("ss")
    st = pr.sem("st")
    sp = pr.sem("sp")
    so_x3 = [pr.sem("so_x30"), pr.sem("so_x31")]

    pr.on("vector", lambda e: e.memset(ones_col_b[:], 1.0))
    pr.on("vector", lambda e: e.memset(ones_col_f[:], 1.0))
    pr.on("vector", lambda e: e.memset(ones_row1[:], 1.0))
    pr.on("vector", lambda e: e.memset(negor[:], -1.0))
    pr.on("vector", lambda e: e.memset(onesf[:], 1.0), sv)
    ones_rdy = sv.n
    pr.wait("gpsimd", sv, ones_rdy)
    idf = pr.on("gpsimd", lambda e: e.affine_select(
        identf[:], onesf[:], [[-1, 16]], ALU.is_equal, 0.0,
        base=0, channel_multiplier=1), sp)
    lpm = pr.on("sync", lambda e: e.dma_start(out=pmf[:], in_=pm_in), ld_pm, 16)
    pr.wait("vector", ld_pm, lpm)
    pr.on("vector", lambda e: e.tensor_copy(pmb[:], pmf[:]), sv)
    pmb_rdy = sv.n
    pr.wait("tensor", sv, pmb_rdy)
    tsp_free = [0, 0]
    for h in range(NH):
        j = h % 2
        pr.wait("tensor", sv, tsp_free[j])
        pr.on("tensor", lambda e, h=h, j=j: e.transpose(
            tsp[j][:, 0:16], pmb[0:16, 128 * h:128 * (h + 1)],
            ident[0:16, 0:16]), st)
        tn = st.n
        pr.wait("vector", st, tn)
        tsp_free[j] = pr.on("vector", lambda e, h=h, j=j: e.tensor_copy(
            kpmT[:, 16 * h:16 * (h + 1)], tsp[j][:, 0:16]), sv)
    kpm_rdy = sv.n
    pr.wait("vector", sv, pmb_rdy)
    pr.on("vector", lambda e: e.tensor_tensor(
        sqp[:, 0:2048], pmf[:, 0:2048], pmf[:, 0:2048], ALU.mult), sv)
    pr.wait("vector", sv)
    pr.on("vector", lambda e: e.tensor_reduce(
        pmn[:], sqp[:, 0:2048].rearrange("p (h d) -> p h d", h=16), AX.X, ALU.add), sv)
    pr.wait("vector", sv)
    pr.on("vector", lambda e: e.reduce_max(pmn1[:], pmn[:], axis=AX.X), sv)
    pmx = sv.n
    pr.wait("tensor", sv, pmx)
    pr.wait("tensor", sp, idf)
    pr.on("tensor", lambda e: e.transpose(B[0:1, 1536:1552], pmn1[:], identf[:]), st)
    tpmx = st.n
    pr.wait("vector", st, tpmx)
    pr.on("vector", lambda e: e.reduce_max(mkrow[0:1, 33:34], B[0:1, 1536:1552],
                                           axis=AX.X), sv)
    pmmax_rdy = sv.n

    blocks = blocks_of(core)
    j_st = j_sv = j_ss = 0
    pt_par = [0]
    for qbi, b in enumerate(blocks):
        g0 = BLK * b
        kt_lo = max(0, 4 * b - 4)
        kt_hi = 4 * b + 4
        nkt = kt_hi - kt_lo
        W = 128 * nkt
        k0 = 128 * kt_lo
        for se, vv in (("st", j_st), ("sv", j_sv), ("ss", j_ss)):
            pr.wait("sync", {"st": st, "sv": sv, "ss": ss}[se], vv)
        lq = pr.on("sync", lambda e, qbi=qbi: e.dma_start(
            out=qsb[:].rearrange("p (a c) -> p a c", a=NH),
            in_=qf_in[:, BLK * qbi:BLK * (qbi + 1)].rearrange("(a p) c -> p a c", p=128)),
            ld_q, 16)
        lk = pr.on("sync", lambda e, k0=k0, W=W: e.dma_start(
            out=kw[:].rearrange("p (a c) -> p a c", a=NH)[:, :, 0:W],
            in_=kf_in[:, k0:k0 + W].rearrange("(a p) c -> p a c", p=128)), ld_k, 16)
        lv = pr.on("sync", lambda e, k0=k0, W=W, nkt=nkt: e.dma_start(
            out=vw[:].rearrange("p (kv d) -> p kv d", d=D)[:, 0:nkt, :],
            in_=v_in[k0:k0 + W, :].rearrange("(kv p) d -> p kv d", p=128)), ld_v, 16)

        pr.wait("vector", sv)
        pr.on("vector", lambda e: e.memset(mkrow[0:1, 0:33], 0.0), sv)
        mz = sv.n
        sq_free = [0, 0]
        nm_free = [0, 0]
        pr.wait("vector", ld_k, lk)
        for h in range(NH):
            for w2 in range(W // BLK):
                j = (2 * h + w2) % 2
                pr.wait("vector", st, sq_free[j])
                tt = pr.on("vector", lambda e, h=h, w2=w2, j=j: e.tensor_tensor(
                    sqk[j][:], kw[:].rearrange("p (a c) -> p a c", a=NH)[:, h, BLK * w2:BLK * (w2 + 1)],
                    kw[:].rearrange("p (a c) -> p a c", a=NH)[:, h, BLK * w2:BLK * (w2 + 1)],
                    ALU.mult), sv)
                pr.wait("tensor", sv, max(tt, nm_free[j]))
                mm = pr.on("tensor", lambda e, j=j: e.matmul(
                    nm_ps[j][:, 0:BLK], ones_col_f[:], sqk[j][:],
                    start=True, stop=True), st)
                sq_free[j] = mm
                pr.wait("vector", st, mm)
                nm_free[j] = pr.on("vector", lambda e, h=h, w2=w2, j=j: e.reduce_max(
                    mkrow[0:1, 2 * h + w2:2 * h + w2 + 1], nm_ps[j][:, 0:BLK],
                    axis=AX.X), sv)
        pr.wait("vector", sv)
        pr.wait("vector", sv, max(mz, pmmax_rdy))
        pr.on("vector", lambda e: e.reduce_max(mkrow[0:1, 32:33], mkrow[0:1, 0:32],
                                               axis=AX.X), sv)
        pr.wait("vector", sv)
        pr.on("vector", lambda e: e.tensor_tensor(
            mkrow[0:1, 32:33], mkrow[0:1, 32:33], mkrow[0:1, 33:34], ALU.max), sv)
        mk_rdy = sv.n

        s_free = [0, 0]
        pt_free = [0, 0]
        o_free = [0, 0]
        o_stop = [0, 0]
        l_free = 0
        lstg_free = [0, 0]
        pmbank_free = 0
        ptm_free = 0
        crow_free = 0
        crow_rdy = 0
        bc0_free = 0
        xo_done = {}
        pr.wait("tensor", sv, kpm_rdy)
        pr.wait("tensor", ld_q, lq)
        pr.wait("tensor", ld_k, lk)
        pr.wait("tensor", ld_v, lv)
        pr.wait("vector", ld_q, lq)
        for h in range(NH):
            hp = h % 2
            jn = h % 2
            pr.wait("vector", st, sq_free[jn])
            tt = pr.on("vector", lambda e, h=h, jn=jn: e.tensor_tensor(
                sqk[jn][:], qsb[:].rearrange("p (a c) -> p a c", a=NH)[:, h, :],
                qsb[:].rearrange("p (a c) -> p a c", a=NH)[:, h, :], ALU.mult), sv)
            pr.wait("tensor", sv, max(tt, nm_free[jn]))
            qmm = pr.on("tensor", lambda e, jn=jn: e.matmul(
                nm_ps[jn][:, 0:BLK], ones_col_f[:], sqk[jn][:],
                start=True, stop=True), st)
            sq_free[jn] = qmm
            pr.wait("vector", st, max(qmm, crow_free))
            pr.wait("vector", sv, mk_rdy)
            nm_free[jn] = pr.on("vector", lambda e, jn=jn: e.tensor_scalar(
                crow[:], nm_ps[jn][:, 0:BLK], mkrow[0:1, 32:33], None, ALU.mult), sv)
            tcr = nm_free[jn]
            pr.wait("scalar", sv, tcr)
            ln1 = pr.on("scalar", lambda e: e.activation(
                crow[:], crow[:], AF.Ln, bias=1e-12), ss)
            pr.wait("scalar", ss, ln1)
            crow_rdy = pr.on("scalar", lambda e: e.activation(
                crow[:], crow[:], AF.Exp, scale=0.5), ss)
            pr.wait("tensor", sv, pmbank_free)
            pr.on("tensor", lambda e, h=h: e.matmul(
                B[0:16, 1536:2048],
                kpmT[:, 16 * h:16 * (h + 1)],
                qsb[:].rearrange("p (a c) -> p a c", a=NH)[:, h, :],
                start=True, stop=False), None)
            pr.wait("tensor", ss, crow_rdy)
            pr.on("tensor", lambda e: e.matmul(
                B[0:16, 1536:2048], negor[0:1, 0:16], crow[:],
                start=False, stop=True), st)
            pmm = st.n
            pr.wait("scalar", st, max(pmm, ptm_free))
            exm = pr.on("scalar", lambda e: e.activation(
                ptm[:], B[0:16, 1536:2048], AF.Exp, scale=SCL), ss)
            pr.wait("tensor", ss, exm)
            pr.wait("tensor", sv, max(l_free, o_free[hp]))
            pr.on("tensor", lambda e: e.matmul(
                l_ps[:, 0:BLK], ones_col_b[0:16, :], ptm[:],
                start=True, stop=False), None)
            pr.on("tensor", lambda e, h=h, hp=hp: e.matmul(
                o_ps[hp][:, 0:BLK], pmb[0:16, 2048 + 128 * h:2048 + 128 * (h + 1)],
                ptm[:], start=True, stop=False), st)
            ptm_free = st.n
            kts = list(range(kt_lo, kt_hi))
            groups = [kts[i:i + 2] for i in range(0, len(kts), 2)]
            for grp in groups:
                jj = pt_par[0] % 2
                pt_par[0] += 1
                pr.wait("tensor", ss, s_free[jj])
                for gi, kt in enumerate(grp):
                    pr.on("tensor", lambda e, gi=gi, kt=kt, jj=jj, h=h, kt_lo=kt_lo: e.matmul(
                        s_ps[jj][:, 512 * gi:512 * (gi + 1)],
                        kw[:].rearrange("p (a c) -> p a c", a=NH)
                        [:, h, 128 * (kt - kt_lo):128 * (kt - kt_lo) + 128],
                        qsb[:].rearrange("p (a c) -> p a c", a=NH)[:, h, :],
                        start=True, stop=False), None)
                    pr.on("tensor", lambda e, gi=gi, jj=jj: e.matmul(
                        s_ps[jj][:, 512 * gi:512 * (gi + 1)],
                        negor[:], crow[:], start=False, stop=True),
                        st if gi == len(grp) - 1 else None)
                smm = st.n
                pr.wait("scalar", st, max(smm, pt_free[jj]))
                ex = pr.on("scalar", lambda e, jj=jj, n=len(grp): e.activation(
                    pt[jj][:, 0:512 * n], s_ps[jj][:, 0:512 * n], AF.Exp,
                    scale=SCL), ss)
                s_free[jj] = ex
                mask_mark = 0
                for gi, kt in enumerate(grp):
                    pr.wait("gpsimd", ss, ex)
                    if kt >= 4 * b:
                        mask_mark = pr.on("gpsimd", lambda e, jj=jj, gi=gi, kt=kt, g0=g0: e.affine_select(
                            pt[jj][:, 512 * gi:512 * (gi + 1)],
                            pt[jj][:, 512 * gi:512 * (gi + 1)],
                            [[1, BLK]], ALU.is_ge, 0.0,
                            base=g0 - 128 * kt, channel_multiplier=-1), sp)
                    else:
                        mask_mark = pr.on("gpsimd", lambda e, jj=jj, gi=gi, kt=kt, g0=g0: e.affine_select(
                            pt[jj][:, 512 * gi:512 * (gi + 1)],
                            pt[jj][:, 512 * gi:512 * (gi + 1)],
                            [[-1, BLK]], ALU.is_ge, 0.0,
                            base=512 - g0 + 128 * kt, channel_multiplier=1), sp)
                pr.wait("tensor", ss, ex)
                pr.wait("tensor", sp, mask_mark)
                for gi, kt in enumerate(grp):
                    last = kt == kt_hi - 1
                    pr.on("tensor", lambda e, jj=jj, gi=gi, last=last: e.matmul(
                        l_ps[:, 0:BLK], ones_col_b[:], pt[jj][:, 512 * gi:512 * (gi + 1)],
                        start=False, stop=last), None)
                    pr.on("tensor", lambda e, jj=jj, gi=gi, kt=kt, h=h, hp=hp, last=last,
                          kt_lo=kt_lo: e.matmul(
                        o_ps[hp][:, 0:BLK],
                        vw[:].rearrange("p (kv d) -> p kv d", d=D)
                        [:, kt - kt_lo, 128 * h:128 * (h + 1)],
                        pt[jj][:, 512 * gi:512 * (gi + 1)],
                        start=False, stop=last), st)
                pt_free[jj] = st.n
            o_stop[hp] = st.n
            crow_free = st.n
            hd = h % 2
            pr.wait("vector", st, o_stop[hp])
            pr.wait("vector", st, lstg_free[hd])
            dl = pr.on("vector", lambda e, hd=hd: e.tensor_copy(
                lstg[hd][:], l_ps[:, 0:BLK]), sv)
            l_free = dl
            pr.wait("vector", sv, dl)
            rc = pr.on("vector", lambda e, hd=hd: e.reciprocal(
                lstg[hd][:], lstg[hd][:]), sv)
            pr.wait("tensor", sv, rc)
            pr.wait("tensor", ss, exm)
            pr.on("tensor", lambda e, hd=hd: e.matmul(
                B[:, 1536:2048], ones_row1[:], lstg[hd][:],
                start=True, stop=True), st)
            bmm = st.n
            lstg_free[hd] = bmm
            pr.wait("vector", st, bmm)
            pr.wait("vector", sv, bc0_free)
            bev = pr.on("vector", lambda e: e.tensor_copy(bc0[:], B[:, 1536:2048]), sv)
            pmbank_free = bev
            pr.wait("vector", sv, bev)
            xon = pr.on("vector", lambda e, h=h, hp=hp: e.tensor_tensor(
                xo[h][:], o_ps[hp][:, 0:BLK], bc0[:], ALU.mult), sv)
            o_free[hp] = xon
            bc0_free = xon
            xo_done[h] = xon
        xo_rdy = sv.n

        if debug and qbi == 0:
            pr.wait("gpsimd", sv, xo_rdy)
            for h in range(NH):
                pr.on("gpsimd", lambda e, h=h: e.dma_start(
                    out=dbg_xo[128 * h:128 * (h + 1), :], in_=xo[h][:]), so_dbg, 16)

        wp_free = 0
        wpf_free = 0
        pj_free = [0, 0]
        xsl_free = [0, 0]
        x3_free = [0, 0]
        for sl in range(D // PSL):
            pr.wait("sync", ss, wpf_free)
            lwp = pr.on("sync", lambda e, sl=sl: e.dma_start(
                out=wpf[:].rearrange("p (a c) -> p a c", a=ND),
                in_=wo_in[:, PSL * sl:PSL * (sl + 1)].rearrange("(a p) c -> p a c", p=128)), ld_wp, 16)
            pr.wait("scalar", ld_wp, lwp)
            pr.wait("scalar", st, wp_free)
            cwp = pr.on("scalar", lambda e: e.copy(wpb[:], wpf[:]), ss)
            wpf_free = cwp
            for qc in range(4):
                pp = qc % 2
                pr.wait("tensor", ss, cwp)
                pr.wait("tensor", sv, max(xo_rdy, pj_free[pp]))
                for h in range(NH):
                    pr.on("tensor", lambda e, h=h, qc=qc, pp=pp: e.matmul(
                        pj_ps[pp][:, 0:PSL],
                        xo[h][:, 128 * qc:128 * (qc + 1)],
                        wpb[:].rearrange("p (a c) -> p a c", a=ND)[:, h, :],
                        start=(h == 0), stop=(h == NH - 1)), st if h == NH - 1 else None)
                pjmm = st.n
                pr.wait("sync", sv, xsl_free[pp])
                lxs = pr.on("sync", lambda e, sl=sl, qc=qc, qbi=qbi, pp=pp: e.dma_start(
                    out=xsl[pp][:], in_=x_in[BLK * qbi + 128 * qc:BLK * qbi + 128 * (qc + 1),
                                             PSL * sl:PSL * (sl + 1)]), ld_xs[pp], 16)
                pr.wait("vector", ld_xs[pp], lxs)
                pr.wait("vector", st, pjmm)
                pr.wait("vector", so_x3[pp], x3_free[pp])
                ev = pr.on("vector", lambda e, pp=pp: e.tensor_tensor(
                    x3st[pp][:], pj_ps[pp][:, 0:PSL], xsl[pp][:], ALU.add), sv)
                pj_free[pp] = ev
                xsl_free[pp] = ev
                pr.wait("gpsimd", sv, ev)
                x3_free[pp] = pr.on("gpsimd", lambda e, sl=sl, qc=qc, qbi=qbi, pp=pp: e.dma_start(
                    out=x3_out[BLK * qbi + 128 * qc:BLK * qbi + 128 * (qc + 1),
                               PSL * sl:PSL * (sl + 1)], in_=x3st[pp][:]), so_x3[pp], 16)
            wp_free = st.n

        pr.wait("tensor", sv)
        pr.wait("tensor", ss)
        pr.wait("tensor", sp)
        for s_ in (so_x3[0], so_x3[1]):
            pr.wait("tensor", s_)
        j_st, j_sv, j_ss = st.n, sv.n, ss.n

    for s_ in (so_x3[0], so_x3[1]) + ((so_dbg,) if debug else ()):
        pr.wait("gpsimd", s_)
    return pr.emit_and_close()



def build_l1_deq():
    pr = new_prog()
    nc = pr.nc
    g_in = nc.dram_tensor("g_pk", [5632, 256], I32, kind="ExternalInput").ap()
    u_in = nc.dram_tensor("u_pk", [5632, 256], I32, kind="ExternalInput").ap()
    d_in = nc.dram_tensor("d_pk", [D, 704], I32, kind="ExternalInput").ap()
    sg_in = nc.dram_tensor("sgT", [128, 44], F32, kind="ExternalInput").ap()
    su_in = nc.dram_tensor("suT", [128, 44], F32, kind="ExternalInput").ap()
    sd_in = nc.dram_tensor("sdT", [128, 16], F32, kind="ExternalInput").ap()
    gT_out = nc.dram_tensor("gT", [256, 5632], BF16, kind="ExternalOutput").ap()
    uT_out = nc.dram_tensor("uT", [256, 5632], BF16, kind="ExternalOutput").ap()
    dT_out = nc.dram_tensor("dT", [704, D], BF16, kind="ExternalOutput").ap()

    ident = add_ident(pr)
    gq = pr.sb("gq", [128, 44 * 256], I32)
    uq = pr.sb("uq", [128, 44 * 256], I32)
    dq = pr.sb("dq", [128, 16 * 704], I32)
    sgT = pr.sb("sgTs", [128, 44], F32)
    suT = pr.sb("suTs", [128, 44], F32)
    sdT = pr.sb("sdTs", [128, 16], F32)
    bgT = pr.sb("bgT", [128, 44], F32)
    buT = pr.sb("buT", [128, 44], F32)
    bdT = pr.sb("bdT", [128, 16], F32)
    cb = [pr.sb(f"cb{i}", [128, 704], BF16) for i in range(2)]
    otg = [pr.sb(f"otg{i}", [128, 128], BF16) for i in range(2)]
    tp_ps = [pr.ps(f"tp{i}", [128, 128], BF16) for i in range(2)]

    ld = pr.sem("ld")
    sv = pr.sem("sv")
    ss = pr.sem("ss")
    st = pr.sem("st")
    so = [pr.sem("so0"), pr.sem("so1")]

    for dst, src, na in ((gq, g_in, 44), (uq, u_in, 44), (dq, d_in, 16)):
        pr.on("sync", lambda e, dst=dst, src=src, na=na: e.dma_start(
            out=dst[:].rearrange("p (a w) -> p a w", a=na),
            in_=src.rearrange("(a p) w -> p a w", p=128)), ld, 16)
    for dst, src in ((sgT, sg_in), (suT, su_in), (sdT, sd_in)):
        pr.on("sync", lambda e, dst=dst, src=src: e.dma_start(out=dst[:], in_=src), ld, 16)
    lall = ld.n
    pr.wait("vector", ld, lall)
    for b_, s_t in ((bgT, sgT), (buT, suT), (bdT, sdT)):
        pr.on("vector", lambda e, b_=b_, s_t=s_t: e.tensor_scalar(
            b_[:], s_t[:], -128.0, None, ALU.mult), sv)
    b_rdy = sv.n
    pr.wait("scalar", sv, b_rdy)
    pr.wait("scalar", ld, lall)

    cb_free = [0, 0]
    otg_free = [0, 0]
    tp_free = [0, 0]
    jobs = []
    for a in range(44):
        jobs.append(("g", a, gq, sgT, bgT, 256, 44, gT_out))
    for a in range(44):
        jobs.append(("u", a, uq, suT, buT, 256, 44, uT_out))
    for a in range(16):
        jobs.append(("d", a, dq, sdT, bdT, 704, 16, dT_out))
    for ji, (kind, a, qb_, sT, bT, wdt, na, outap) in enumerate(jobs):
        m = ji % 2
        pr.wait("scalar", st, cb_free[m])
        cv = pr.on("scalar", lambda e, qb_=qb_, sT=sT, bT=bT, a=a, m=m, wdt=wdt, na=na: e.activation(
            cb[m][:, 0:wdt],
            qb_[:].rearrange("p (a w) -> p a w", a=na)[:, a, :],
            AF.Identity, scale=sT[:, a:a + 1], bias=bT[:, a:a + 1]), ss)
        nj = (wdt + 127) // 128
        for j in range(nj):
            jm = (ji * 6 + j) % 2
            jw = min(128, wdt - 128 * j)
            pr.wait("tensor", ss, cv)
            pr.wait("tensor", sv, tp_free[jm])
            pr.on("tensor", lambda e, m=m, j=j, jm=jm, jw=jw: e.transpose(
                tp_ps[jm][0:jw, :], cb[m][:, 128 * j:128 * j + jw], ident[:]), st)
            tn = st.n
            cb_free[m] = tn
            pr.wait("vector", st, tn)
            pr.wait("vector", so[jm], otg_free[jm])
            ev = pr.on("vector", lambda e, jm=jm, jw=jw: e.tensor_copy(
                otg[jm][0:jw, :], tp_ps[jm][0:jw, :]), sv)
            tp_free[jm] = ev
            pr.wait("gpsimd", sv, ev)
            otg_free[jm] = pr.on("gpsimd", lambda e, outap=outap, a=a, j=j, jm=jm, jw=jw: e.dma_start(
                out=outap[128 * j:128 * j + jw, 128 * a:128 * (a + 1)],
                in_=otg[jm][0:jw, :]), so[jm], 16)
    for s_ in so:
        pr.wait("gpsimd", s_)
    return pr.emit_and_close()



def build_l1_ffn():
    pr = new_prog()
    nc = pr.nc
    x_in = nc.dram_tensor("x", [TPC, D], F32, kind="ExternalInput").ap()
    gT_in = nc.dram_tensor("gT", [D, 5632], BF16, kind="ExternalInput").ap()
    uT_in = nc.dram_tensor("uT", [D, 5632], BF16, kind="ExternalInput").ap()
    dT_in = nc.dram_tensor("dT", [5632, D], BF16, kind="ExternalInput").ap()
    x4_out = nc.dram_tensor("x4", [TPC, D], F32, kind="ExternalOutput").ap()

    ident = add_ident(pr)
    pr.constant(LN_EPS)
    NI = 44
    DSL = 256

    xt = [pr.sb(f"xt{a}", [128, BLK], BF16) for a in range(ND)]
    xbuf = [pr.sb(f"xbuf{i}", [128, D], F32) for i in range(2)]
    xh = [pr.sb(f"xh{i}", [128, D], BF16) for i in range(2)]
    sqd = pr.sb("sqd", [128, D], F32)
    stat = [pr.sb(f"stat{i}", [128, 8], F32) for i in range(NT)]
    wg = [pr.sb(f"wg{i}", [128, ND * 128], BF16) for i in range(2)]
    wu = [pr.sb(f"wu{i}", [128, ND * 128], BF16) for i in range(2)]
    hsb = [pr.sb(f"hsb{i}", [128, BLK], BF16) for i in range(NI)]
    gs = [pr.sb(f"gs{i}", [128, BLK], BF16) for i in range(2)]
    dsl = pr.sb("dsl", [128, NI * DSL], BF16)
    xr = [pr.sb(f"xr{i}", [128, DSL], F32) for i in range(2)]
    x4st = [pr.sb(f"x4st{i}", [128, DSL], F32) for i in range(2)]

    tp_ps = [pr.ps(f"tp{i}", [128, 8 * 128], BF16) for i in range(2)]
    gu_ps = pr.ps("gu_ps", [128, 2048])
    g_ps = [gu_ps[:, 0:BLK], gu_ps[:, BLK:1024]]
    u_ps = [gu_ps[:, 1024:1536], gu_ps[:, 1536:2048]]
    dn_ps2 = pr.ps("dn_ps2", [128, 1024])
    dn_ps = [dn_ps2[:, 0:BLK], dn_ps2[:, BLK:1024]]

    ldx = [pr.sem("ldx0"), pr.sem("ldx1")]
    ldg = [pr.sem("ldg0"), pr.sem("ldg1")]
    ldu = [pr.sem("ldu0"), pr.sem("ldu1")]
    ldd = pr.sem("ldd")
    ldr = [pr.sem("ldr0"), pr.sem("ldr1")]
    sv = pr.sem("sv")
    ss = pr.sem("ss")
    st = pr.sem("st")
    so2 = [pr.sem("so20"), pr.sem("so21")]

    x_free = [0, 0]
    xh_free = [0, 0]
    tp_done = {}
    xr_free = [0, 0]
    x4_free = [0, 0]
    wg_free = [0, 0]
    wu_free = [0, 0]
    gps_free = [0, 0]
    ups_free = [0, 0]
    gs_free = [0, 0]
    dn_free = [0, 0]
    dsl_free = 0
    hsb_last = 0

    for hf in range(2):
        for i4 in range(4):
            i = 4 * hf + i4
            m = i % 2
            pr.wait("sync", sv, x_free[m])
            lx = pr.on("sync", lambda e, i=i, m=m: e.dma_start(
                out=xbuf[m][:], in_=x_in[128 * i:128 * (i + 1), :]), ldx[m], 16)
            pr.wait("vector", ldx[m], lx)
            nred = pr.on("vector", lambda e, m=m, i=i: e.reduce_sum(
                stat[i][:, 0:1], xbuf[m][:], axis=AX.X), sv)
            pr.wait("vector", sv, nred)
            pr.on("vector", lambda e, i=i: e.tensor_scalar(
                stat[i][:, 2:3], stat[i][:, 0:1], -1.0 / D, None, ALU.mult), sv)
            pr.wait("scalar", ldx[m], lx)
            pr.on("scalar", lambda e, m=m, i=i: e.activation(
                sqd[:], xbuf[m][:], AF.Square, scale=1.0 / math.sqrt(D),
                accum_out=stat[i][:, 1:2]))
            pr.wait("scalar", sv)
            pr.on("scalar", lambda e, i=i: e.activation(
                stat[i][:, 3:4], stat[i][:, 2:3], AF.Square), ss)
            pr.wait("vector", ss)
            pr.on("vector", lambda e, i=i: e.tensor_tensor(
                stat[i][:, 4:5], stat[i][:, 1:2], stat[i][:, 3:4], ALU.subtract), sv)
            pr.wait("scalar", sv)
            nln = pr.on("scalar", lambda e, i=i: e.activation(
                stat[i][:, 5:6], stat[i][:, 4:5], AF.Ln, bias=LN_EPS), ss)
            pr.wait("scalar", ss, nln)
            pr.on("scalar", lambda e, i=i: e.activation(
                stat[i][:, 5:6], stat[i][:, 5:6], AF.Exp, scale=-0.5), ss)
            pr.wait("vector", ss)
            pr.wait("vector", st, xh_free[m])
            xhn = pr.on("vector", lambda e, m=m, i=i: e.tensor_scalar(
                xh[m][:], xbuf[m][:], stat[i][:, 2:3], stat[i][:, 5:6], ALU.add, ALU.mult), sv)
            x_free[m] = xhn
            pr.wait("tensor", sv, xhn)
            for grp in range(2):
                pr.wait("tensor", sv, tp_done.get((i, grp - 1) if grp else (i - 1, 1), 0))
                for a8 in range(8):
                    a = grp * 8 + a8
                    pr.on("tensor", lambda e, m=m, a=a, a8=a8, grp=grp: e.transpose(
                        tp_ps[grp][:, 128 * a8:128 * (a8 + 1)],
                        xh[m][:, 128 * a:128 * (a + 1)], ident[:]),
                        st if a8 == 7 else None)
                tg = st.n
                pr.wait("vector", st, tg)
                for a8 in range(8):
                    a = grp * 8 + a8
                    pr.on("vector", lambda e, i4=i4, a=a, a8=a8, grp=grp: e.tensor_copy(
                        xt[a][:, 128 * i4:128 * (i4 + 1)],
                        tp_ps[grp][:, 128 * a8:128 * (a8 + 1)]),
                        sv if a8 == 7 else None)
                tp_done[(i, grp)] = sv.n
            xh_free[m] = st.n
        xt_rdy = sv.n

        for ic in range(NI):
            m = ic % 2
            pr.wait("sync", st, wg_free[m])
            lg = pr.on("sync", lambda e, ic=ic, m=m: e.dma_start(
                out=wg[m][:].rearrange("p (a c) -> p a c", a=ND),
                in_=gT_in[:, 128 * ic:128 * (ic + 1)].rearrange("(a p) c -> p a c", p=128)),
                ldg[m], 16)
            pr.wait("sync", st, wu_free[m])
            lu = pr.on("sync", lambda e, ic=ic, m=m: e.dma_start(
                out=wu[m][:].rearrange("p (a c) -> p a c", a=ND),
                in_=uT_in[:, 128 * ic:128 * (ic + 1)].rearrange("(a p) c -> p a c", p=128)),
                ldu[m], 16)
            pr.wait("tensor", sv, xt_rdy)
            pr.wait("tensor", ldg[m], lg)
            pr.wait("tensor", ss, gps_free[m])
            for a in range(ND):
                pr.on("tensor", lambda e, a=a, m=m: e.matmul(
                    g_ps[m][:, 0:BLK],
                    wg[m][:].rearrange("p (a c) -> p a c", a=ND)[:, a, :],
                    xt[a][:], start=(a == 0), stop=(a == ND - 1)),
                    st if a == ND - 1 else None)
            gmm = st.n
            pr.wait("tensor", ldu[m], lu)
            pr.wait("tensor", sv, ups_free[m])
            for a in range(ND):
                pr.on("tensor", lambda e, a=a, m=m: e.matmul(
                    u_ps[m][:, 0:BLK],
                    wu[m][:].rearrange("p (a c) -> p a c", a=ND)[:, a, :],
                    xt[a][:], start=(a == 0), stop=(a == ND - 1)),
                    st if a == ND - 1 else None)
            umm = st.n
            wg_free[m] = gmm
            wu_free[m] = umm
            pr.wait("scalar", st, gmm)
            pr.wait("scalar", sv, gs_free[m])
            sl = pr.on("scalar", lambda e, m=m: e.activation(
                gs[m][:], g_ps[m][:, 0:BLK], AF.Silu), ss)
            gps_free[m] = sl
            pr.wait("vector", ss, sl)
            pr.wait("vector", st, umm)
            hm = pr.on("vector", lambda e, ic=ic, m=m: e.tensor_tensor(
                hsb[ic][:], gs[m][:], u_ps[m][:, 0:BLK], ALU.mult), sv)
            ups_free[m] = hm
            gs_free[m] = hm
        hsb_last = sv.n

        for dc in range(D // DSL):
            pr.wait("sync", st, dsl_free)
            ldn = pr.on("sync", lambda e, dc=dc: e.dma_start(
                out=dsl[:].rearrange("p (a c) -> p a c", a=NI),
                in_=dT_in[:, DSL * dc:DSL * (dc + 1)].rearrange("(a p) c -> p a c", p=128)),
                ldd, 16)
            for t2 in range(4):
                pp = (4 * dc + t2) % 2
                pr.wait("tensor", ldd, ldn)
                pr.wait("tensor", sv, max(hsb_last, dn_free[pp]))
                for icc in range(NI):
                    pr.on("tensor", lambda e, icc=icc, t2=t2, pp=pp: e.matmul(
                        dn_ps[pp][:, 0:DSL],
                        hsb[icc][:, 128 * t2:128 * (t2 + 1)],
                        dsl[:].rearrange("p (a c) -> p a c", a=NI)[:, icc, :],
                        start=(icc == 0), stop=(icc == NI - 1)),
                        st if icc == NI - 1 else None)
                dmm = st.n
                m = pp
                pr.wait("sync", sv, xr_free[m])
                lxr = pr.on("sync", lambda e, hf=hf, t2=t2, dc=dc, m=m: e.dma_start(
                    out=xr[m][:], in_=x_in[BLK * hf + 128 * t2:BLK * hf + 128 * (t2 + 1),
                                           DSL * dc:DSL * (dc + 1)]), ldr[m], 16)
                pr.wait("vector", ldr[m], lxr)
                pr.wait("vector", st, dmm)
                pr.wait("vector", so2[m], x4_free[m])
                ev = pr.on("vector", lambda e, pp=pp, m=m: e.tensor_tensor(
                    x4st[m][:], dn_ps[pp][:, 0:DSL], xr[m][:], ALU.add), sv)
                dn_free[pp] = ev
                xr_free[m] = ev
                pr.wait("gpsimd", sv, ev)
                x4_free[m] = pr.on("gpsimd", lambda e, hf=hf, t2=t2, dc=dc, m=m: e.dma_start(
                    out=x4_out[BLK * hf + 128 * t2:BLK * hf + 128 * (t2 + 1),
                               DSL * dc:DSL * (dc + 1)], in_=x4st[m][:]), so2[m], 16)
            dsl_free = st.n
        pr.wait("tensor", sv)
        pr.wait("vector", st)
        xt_rdy = 0

    for s_ in (so2[0], so2[1]):
        pr.wait("gpsimd", s_)
    return pr.emit_and_close()



def shard_tokens(x):
    out = []
    for c in range(NC):
        b0, b1 = blocks_of(c)
        out.append(np.concatenate([x[BLK * b0:BLK * (b0 + 1)], x[BLK * b1:BLK * (b1 + 1)]], 0))
    return out


def unshard_tokens(shards):
    seq = np.empty((SEQ,) + shards[0].shape[1:], shards[0].dtype)
    for c in range(NC):
        b0, b1 = blocks_of(c)
        seq[BLK * b0:BLK * (b0 + 1)] = shards[c][:BLK]
        seq[BLK * b1:BLK * (b1 + 1)] = shards[c][BLK:]
    return seq


def gather_cols(shards):
    R = shards[0].shape[0]
    out = np.empty((R, SEQ), shards[0].dtype)
    for b in range(NBLK):
        c = b if b < NC else NBLK - 1 - b
        h = 0 if b < NC else 1
        out[:, BLK * b:BLK * (b + 1)] = shards[c][:, BLK * h:BLK * (h + 1)]
    return out


def _f16(a):
    a = np.asarray(a)
    return a if a.dtype == np.float16 else a.view(np.float16)


def kernel(**inputs):
    x = np.asarray(inputs["L0_attn_x"], np.float32)[0]
    wqkv0 = np.asarray(inputs["L0_attn_Wqkv"], np.float32)
    wproj0 = np.asarray(inputs["L0_attn_Wproj"], np.float32)
    lq1 = np.asarray(inputs["L0_attn_lambda_q1"], np.float32).reshape(1, 64)
    lk1 = np.asarray(inputs["L0_attn_lambda_k1"], np.float32).reshape(1, 64)
    lq2 = np.asarray(inputs["L0_attn_lambda_q2"], np.float32).reshape(1, 64)
    lk2 = np.asarray(inputs["L0_attn_lambda_k2"], np.float32).reshape(1, 64)
    gup = np.asarray(inputs["L0_ffn_gate_up_packed"], np.int32)
    gsc = _f16(inputs["L0_ffn_gate_up_scales"])
    dnp = np.asarray(inputs["L0_ffn_down_packed"], np.int32)
    dsc = _f16(inputs["L0_ffn_down_scales"])
    wn = np.asarray(inputs["L1_attn_w_norm"], np.float32)
    wqkv1 = np.asarray(inputs["L1_attn_w_qkv"], np.float32)
    wout1 = np.asarray(inputs["L1_attn_w_out"], np.float32)
    pmem = np.asarray(inputs["L1_attn_pmem"], np.float32)
    wqg = np.asarray(inputs["L1_ffn_wq_gate"], np.int32)
    sg = np.asarray(inputs["L1_ffn_s_gate"], np.float32)
    wqu = np.asarray(inputs["L1_ffn_wq_up"], np.int32)
    su = np.asarray(inputs["L1_ffn_s_up"], np.float32)
    wqd = np.asarray(inputs["L1_ffn_wq_down"], np.int32)
    sd = np.asarray(inputs["L1_ffn_s_down"], np.float32)
    cores = list(range(NC))

    shards = shard_tokens(x)
    r1 = run_bass_kernel_spmd(build_l0_qkv(),
                              [{"x": s, "wqkv": wqkv0} for s in shards],
                              core_ids=cores)
    qf = [r["qf"] for r in r1.results]
    kf_full = gather_cols([r["kf"] for r in r1.results])
    v_full = unshard_tokens([r["v"] for r in r1.results])
    x1_shards = []
    for c in cores:
        im = {"qf": qf[c], "kf": kf_full, "v": v_full,
              "lq1": lq1, "lk1": lk1, "lq2": lq2, "lk2": lk2,
              "wproj": wproj0, "x": shards[c]}
        r2 = run_bass_kernel_spmd(build_l0_attn(c), [im], core_ids=[c])
        x1_shards.append(r2.results[0]["x1"])

    in3 = []
    for c in cores:
        dn_sc_c = np.stack([dsc[min((11 * c + j) // 2, 43)] for j in range(11)], 0)
        in3.append({
            "gup_pk": np.ascontiguousarray(gup[:, 32 * c:32 * (c + 1)]),
            "gup_sc": np.ascontiguousarray(gsc[2 * c:2 * (c + 1), :]),
            "dn_pk": np.ascontiguousarray(dnp[:, 88 * c:88 * (c + 1)]),
            "dn_sc": np.ascontiguousarray(dn_sc_c),
        })
    r3 = run_bass_kernel_spmd(build_l0_deq(), in3, core_ids=cores)
    gupT = np.concatenate([r["gupT"] for r in r3.results], 0)
    dnT = np.concatenate([r["dnT"] for r in r3.results], 0)
    r4 = run_bass_kernel_spmd(build_l0_ffn(),
                              [{"x": s, "gupT": gupT, "dnT": dnT} for s in x1_shards],
                              core_ids=cores)
    x2_shards = [r["x2"] for r in r4.results]

    wn16 = np.ascontiguousarray(wn.reshape(16, 128))
    r5 = run_bass_kernel_spmd(build_l1_qkv(),
                              [{"x": s, "wn": wn16, "wqkv": wqkv1} for s in x2_shards],
                              core_ids=cores)
    qf1 = [r["qf"] for r in r5.results]
    kf1 = gather_cols([r["kf"] for r in r5.results])
    v1 = unshard_tokens([r["v"] for r in r5.results])
    pm16 = np.ascontiguousarray(np.transpose(pmem, (2, 0, 1, 3)).reshape(16, 4096))
    x3_shards = []
    for c in cores:
        im = {"qf": qf1[c], "kf": kf1, "v": v1, "pm": pm16,
              "wout": wout1, "x": x2_shards[c]}
        r6 = run_bass_kernel_spmd(build_l1_attn(c), [im], core_ids=[c])
        x3_shards.append(r6.results[0]["x3"])

    in7 = []
    for c in cores:
        in7.append({
            "g_pk": np.ascontiguousarray(wqg[:, 256 * c:256 * (c + 1)]),
            "u_pk": np.ascontiguousarray(wqu[:, 256 * c:256 * (c + 1)]),
            "d_pk": np.ascontiguousarray(wqd[:, 704 * c:704 * (c + 1)]),
            "sgT": np.ascontiguousarray(sg.reshape(44, 128).T),
            "suT": np.ascontiguousarray(su.reshape(44, 128).T),
            "sdT": np.ascontiguousarray(sd.reshape(16, 128).T),
        })
    r7 = run_bass_kernel_spmd(build_l1_deq(), in7, core_ids=cores)
    gT = np.concatenate([r["gT"] for r in r7.results], 0)
    uT = np.concatenate([r["uT"] for r in r7.results], 0)
    dT = np.concatenate([r["dT"] for r in r7.results], 0)
    r8 = run_bass_kernel_spmd(build_l1_ffn(),
                              [{"x": s, "gT": gT, "uT": uT, "dT": dT} for s in x3_shards],
                              core_ids=cores)
    x4 = unshard_tokens([r["x4"] for r in r8.results])
    return np.ascontiguousarray(x4, dtype=np.float32)[None]
